# version 04 (final)
# baseline (speedup 1.0000x reference)
import os
import numpy as np
from contextlib import ExitStack

import concourse.bass as bass
import concourse.mybir as mybir
from concourse.bass_utils import run_bass_kernel_spmd

F32 = mybir.dt.float32
BF16 = mybir.dt.bfloat16
U8 = mybir.dt.uint8
AF = mybir.ActivationFunctionType
ALU = mybir.AluOpType
AX = mybir.AxisListType

NCORES = 8
T = 2048
NT = T // 128
D = 1024
DC = D // 128
NBLOCKS = int(os.environ.get("MK_NBLOCKS", "4"))
BLOCKSET = set(int(x) for x in os.environ.get(
    "MK_BLOCKSET", ",".join(str(i + 1) for i in range(NBLOCKS))).split(",")
    if x)
TAP = os.environ.get("MK_TAP", "")

EPS_LN = 1e-5
EPS_RMS = float(np.finfo(np.float32).eps)
RNE = 12582912.0



class Tok:
    __slots__ = ("sem", "val", "force")
    def __init__(self, sem, val, force=False):
        self.sem, self.val, self.force = sem, val, force


class EngQ:
    def __init__(self, prog, name):
        self.prog = prog
        self.name = name
        self.ops = []
        self.cnt = 0
        self.waited = {}
        self.log = []
        self.drain_at = 0

    def wait(self, tok):
        if tok is None:
            return
        if tok.sem == self.name and not tok.force:
            if self.name != "t" and tok.val > self.drain_at:
                self.drain_at = self.cnt
                self.log.append("drain")
                self.ops.append(lambda e: e.drain())
            return
        if self.waited.get(tok.sem, 0) >= tok.val and not os.environ.get(
                "MK_NOSUBSUME"):
            return
        self.waited[tok.sem] = max(self.waited.get(tok.sem, 0), tok.val)
        sem = self.prog.sems[tok.sem]
        v = tok.val
        self.log.append(f"wait {tok.sem}>={tok.val}")
        if not hasattr(self, "waits_emitted"):
            self.waits_emitted = []
        self.waits_emitted.append((tok.sem, tok.val))
        self.ops.append(lambda e, sem=sem, v=v: e.wait_ge(sem, v))

    def run(self, fn):
        self.log.append("op")
        self.ops.append(fn)

    def done(self, fn, inc=1, force=False):
        sem = self.prog.sems[self.name]
        self.cnt += inc
        self.log.append(f"op+inc->{self.cnt}{getattr(self, 'lbl', '')}")
        self.ops.append(lambda e, fn=fn, sem=sem, inc=inc: fn(e).then_inc(sem, inc))
        return Tok(self.name, self.cnt, force=(force or inc >= 16))


class Buf:
    __slots__ = ("name", "w", "readers")
    def __init__(self, name, readers=None):
        self.name = name
        self.w = None
        self.readers = list(readers or [])


class Prog:
    def __init__(self, nc, ctx):
        self.nc = nc
        self.ctx = ctx
        self.sems = {}
        self.q = {}
        for name in ("t", "v", "s", "g", "y"):
            self.sems[name] = ctx.enter_context(nc.semaphore("sem_" + name))
            self.q[name] = EngQ(self, name)

    def barrier_toks(self):
        return [Tok(n, self.q[n].cnt, force=True) for n in ("t", "v", "s", "g", "y")]

    def task(self, eng, fns, reads=(), writes=(), extra=(), inc=1, force=False,
             label=""):
        q = self.q[eng]
        q.lbl = " #" + label if label else ""
        deps = []
        for b in reads:
            if b.w is not None:
                deps.append(b.w)
            q.wait(b.w)
        for b in writes:
            if b.w is not None:
                deps.append(b.w)
            q.wait(b.w)
            for r in b.readers:
                deps.append(r)
                q.wait(r)
        for tk in extra:
            deps.append(tk)
            q.wait(tk)
        if not hasattr(self, "taskrec"):
            self.taskrec = []
        _deps = [(d.sem, d.val, d.force) for d in deps]
        if inc >= 16:
            tok = None
            for fn in fns:
                if q.cnt > 0:
                    sem = self.sems[eng]
                    v = q.cnt
                    q.log.append(f"selfwait {eng}>={v}")
                    q.ops.append(lambda e, sem=sem, v=v: e.wait_ge(sem, v))
                tok = q.done(fn, inc=16)
        else:
            for fn in fns[:-1]:
                q.run(fn)
            tok = q.done(fns[-1], inc=inc, force=force)
        for b in reads:
            b.readers.append(tok)
        for b in writes:
            b.w = tok
            b.readers = []
        self.taskrec.append((eng, label, _deps, tok.val,
                             getattr(q, "waits_emitted", [])))
        q.waits_emitted = []
        q.lbl = ""
        return tok

    def sbuf(self, name, shape, dt):
        return self.ctx.enter_context(self.nc.sbuf_tensor(name, list(shape), dt))

    def psum(self, name, shape, dt=F32):
        return self.ctx.enter_context(self.nc.psum_tensor(name, list(shape), dt))

    def play(self):
        block = self.ctx.enter_context(self.nc.Block())
        q = self.q
        block.tensor(lambda e: [f(e) for f in q["t"].ops])
        block.vector(lambda e: [f(e) for f in q["v"].ops])
        block.scalar(lambda e: [f(e) for f in q["s"].ops])
        block.gpsimd(lambda e: [f(e) for f in q["g"].ops])
        block.sync(lambda e: [f(e) for f in q["y"].ops])


DTSZ = {F32: 4, BF16: 2, U8: 1}


class Arena:
    def __init__(self, p, name, nbytes):
        self.p = p
        self.t = p.sbuf(name, [128, nbytes], U8)
        self.nbytes = nbytes
        self.off = 0
        self.fence = []

    def alloc(self, name, shape, dt):
        assert len(shape) == 2 and shape[0] <= 128
        n = shape[1] * DTSZ[dt]
        off = (self.off + 63) & ~63
        self.off = off + n
        assert self.off <= self.nbytes, \
            f"arena overflow at {name}: {self.off} > {self.nbytes}"
        ap = self.t.ap()[0:shape[0], off:off + n].bitcast(dt)
        b = Buf(name, readers=self.fence)
        return ap, b

    def reset(self):
        self.off = 0
        self.fence = self.p.barrier_toks()

    def mark(self):
        return self.off

    def release(self, mark, bufs):
        self.off = mark
        toks = []
        for b in bufs:
            if b.w is not None:
                toks.append(b.w)
            toks.extend(b.readers)
        self.fence = self.fence + toks



def build(nc):
    ctx = ExitStack()
    ctx.__enter__()
    p = Prog(nc, ctx)

    dram = {}
    def din(name, shape, dt=F32):
        dram[name] = nc.dram_tensor(name, list(shape), dt, kind="ExternalInput").ap()
        return dram[name]

    x_in = din("x", [T, D])
    out = nc.dram_tensor("out", [T, D], F32, kind="ExternalOutput").ap()
    tapout = nc.dram_tensor("tapout", [128, 16384], F32,
                            kind="ExternalOutput").ap() if TAP else None
    din("coremask", [128, 8])
    din("ident", [128, 128])
    if 1 in BLOCKSET:
        for nm in ("WrT", "WvT", "WwT", "WoT", "Wk"):
            din(nm, [D, D])
        din("selhead", [D, 16])
        din("kexpand", [16, 128 * DC])
    if 2 in BLOCKSET:
        din("w1T", [D, 4096])
        din("w2T", [4096, D])
        din("convw", [128, 32 * 3])
        din("convb", [128, 32])
        din("halomask", [128, 2])
    if 3 in BLOCKSET:
        for nm in ("Wq", "Wk", "Wv", "Wg", "Wout"):
            din("g_" + nm, [D, D])
        din("g_Wgk1", [D, 16])
        din("g_Wgk2", [16, D])
        din("g_bgk2", [128, DC])
        din("g_gamma", [128, D])
        din("g_beta", [128, D])
        din("g_maskT2", [128, 256])
    if 4 in BLOCKSET:
        din("f_wqkv", [2, 1024, 1536])
        din("f_wo", [2, 512, 1024])
        din("f_w1", [2, 1024, 5460])
        din("f_w2", [2, 2730, 1024])
        din("f_b1", [2, 5460])
        din("f_b2", [2, 1024])
        din("f_qs", [2, 128, 512])
        din("f_ks", [2, 128, 512])
        din("f_lg", [2, 128, 8])
        din("f_lb", [2, 128, 8])
        din("f_mask", [5, 128, 640])
        din("f_rot", [2, 20, 128, 2048])
        din("f_hsel", [128, 8])

    x_res = p.sbuf("x_res", [128, NT * D], F32)
    b_xres = Buf("x_res")
    ident_f = p.sbuf("ident_f", [128, 128], F32); b_ident = Buf("ident")
    ones_f32 = p.sbuf("ones_f32", [128, 16], F32); b_ones = Buf("ones")
    ones_row = p.sbuf("ones_row", [1, 128], F32)
    stat = p.sbuf("stat", [128, 128], F32); b_stat = Buf("stat")
    consts = p.sbuf("consts", [128, 32], F32)
    const_cache = {}
    def cconst(val):
        if val not in const_cache:
            k = len(const_cache)
            assert k < 32
            ap = consts[:, k:k + 1]
            b = Buf("const_%r" % val)
            p.task("v", [lambda e, ap=ap, val=val: e.memset(ap, float(val))],
                   writes=[b])
            const_cache[val] = (ap, b)
        return const_cache[val][0]
    def cconst_b(val):
        cconst(val)
        return const_cache[val][1]
    stage = p.sbuf("stage", [128, 2 * D], F32); b_stage = Buf("stage")
    cmask = p.sbuf("cmask", [128, 8], F32); b_cmask = Buf("cmask")

    AR = Arena(p, "arena", 134 * 1024)

    psA = p.psum("psA", [128, 512]); bA = Buf("psA")
    psB = p.psum("psB", [128, 512]); bB = Buf("psB")
    psC = p.psum("psC", [128, 512]); bC = Buf("psC")
    psD = p.psum("psD", [128, 512]); bD = Buf("psD")
    psE = p.psum("psE", [128, 1024]); bE = Buf("psE")
    psF = p.psum("psF", [128, 1024]); bF = Buf("psF")
    BANKS4 = [(psA, bA), (psB, bB), (psC, bC), (psD, bD)]
    BANKS2 = [(psE, bE), (psF, bF)]

    def tap(name, ap, buf=None):
        if not TAP or TAP != name:
            return
        pr, fr = ap.shape[0], ap.shape[1]
        p.task("g",
               [lambda e: e.dma_start(out=tapout[0:pr, 0:fr], in_=ap)],
               reads=[buf] if buf is not None else [],
               extra=p.barrier_toks(), inc=16)

    p.task("y", [lambda e: e.dma_start(
        out=x_res.ap().rearrange("p (i d) -> p i d", d=D),
        in_=x_in.rearrange("(i p) d -> p i d", p=128))],
        writes=[b_xres], inc=16)
    p.task("y", [lambda e: e.dma_start(out=ident_f[:], in_=dram["ident"])],
           writes=[b_ident], inc=16)
    p.task("y", [lambda e: e.dma_start(out=cmask[:], in_=dram["coremask"])],
           writes=[b_cmask], inc=16)
    p.task("v", [lambda e: e.memset(ones_f32[:], 1.0),
                 lambda e: e.memset(ones_row[:], 1.0)], writes=[b_ones])


    def ln_to_fm(src_buf, src_ap_fn, dst_buf, dst_fm):
        scr, b_scr = AR.alloc("ln_scr", [128, D], F32)
        for i in range(NT):
            src = src_ap_fn(i)
            p.task("v", [lambda e, src=src, i=i: e.reduce_sum(
                stat[:, i:i + 1], src, axis=AX.X)],
                reads=[src_buf], writes=[b_stat])
            p.task("s", [lambda e, src=src, i=i: e.activation(
                scr, src, AF.Square, accum_out=stat[:, 16 + i:16 + i + 1])],
                reads=[src_buf], writes=[b_scr, b_stat])
        def chain(e):
            m = stat[:, 32:48]; v2 = stat[:, 48:64]
            e.tensor_scalar(m, stat[:, 0:16], 1.0 / D, None, ALU.mult)
            e.tensor_scalar(v2, stat[:, 16:32], 1.0 / D, None, ALU.mult)
            e.drain()
            e.tensor_tensor(stat[:, 0:16], m, m, ALU.mult)
            e.drain()
            return e.tensor_tensor(v2, v2, stat[:, 0:16], ALU.subtract)
        p.task("v", [chain], reads=[b_stat], writes=[b_stat])
        p.task("s", [lambda e: e.activation(
            stat[:, 48:64], stat[:, 48:64], AF.Sqrt, bias=cconst(EPS_LN))],
            reads=[b_stat, cconst_b(EPS_LN)], writes=[b_stat])
        def chain2(e):
            e.reciprocal(stat[:, 0:16], stat[:, 48:64])
            e.drain()
            return e.scalar_tensor_tensor(
                stat[:, 16:32], stat[:, 32:48], -1.0,
                stat[:, 0:16], ALU.mult, ALU.mult)
        p.task("v", [chain2], reads=[b_stat], writes=[b_stat])
        for i in range(NT):
            src = src_ap_fn(i)
            sl = stage[:, (i % 2) * D:(i % 2) * D + D]
            p.task("s", [lambda e, src=src, i=i, sl=sl: e.activation(
                sl, src, AF.Identity, bias=stat[:, 16 + i:16 + i + 1],
                scale=stat[:, 0 + i:0 + i + 1])],
                reads=[src_buf, b_stat], writes=[b_stage])
            if i == 0:
                tap("stage0", sl, b_stage)
                tap("stat", stat[:, 0:64], b_stat)
            for half in range(2):
                ps, bank = BANKS4[(2 * i + half) % 4]
                def tp(e, sl=sl, half=half, ps=ps):
                    r = None
                    for c in range(4):
                        cc = half * 4 + c
                        r = e.transpose(ps[:, c * 128:(c + 1) * 128],
                                        sl[:, cc * 128:(cc + 1) * 128], ident_f[:])
                    return r
                p.task("t", [tp], reads=[b_stage, b_ident], writes=[bank])
                def ev(e, i=i, half=half, ps=ps):
                    return e.tensor_copy(
                        dst_fm.rearrange("p (c t) -> p c t", t=T)
                        [:, half * 4:half * 4 + 4, i * 128:(i + 1) * 128],
                        ps[:].rearrange("p (c w) -> p c w", w=128))
                p.task("v", [ev], reads=[bank], writes=[dst_buf])

    def ln_residual_from_psum(i, psq, bankq, scale_col, scale_buf, scr, b_scr):
        p.task("v", [lambda e: e.reduce_sum(
            stat[:, 64:65], psq[:, 0:D], axis=AX.X)],
            reads=[bankq], writes=[b_stat])
        p.task("s", [lambda e: e.activation(
            scr[:, 0:D], psq[:, 0:D], AF.Square, accum_out=stat[:, 65:66])],
            reads=[bankq], writes=[b_scr, b_stat])
        def chain(e):
            c = stat
            e.tensor_scalar(c[:, 66:67], c[:, 64:65], 1.0 / D, None, ALU.mult)
            e.tensor_scalar(c[:, 67:68], c[:, 65:66], 1.0 / D, None, ALU.mult)
            e.drain()
            e.tensor_tensor(c[:, 68:69], c[:, 66:67], c[:, 66:67], ALU.mult)
            e.drain()
            e.tensor_tensor(c[:, 67:68], c[:, 67:68], c[:, 68:69], ALU.subtract)
            if scale_col is not None:
                e.tensor_tensor(c[:, 68:69], scale_col, scale_col, ALU.mult)
                e.drain()
                e.tensor_tensor(c[:, 67:68], c[:, 67:68], c[:, 68:69], ALU.mult)
            return e.engine_nop() if False else e.tensor_copy(
                c[:, 71:72], c[:, 67:68])
        p.task("v", [chain], reads=[b_stat] + ([scale_buf] if scale_buf else []),
               writes=[b_stat])
        p.task("s", [lambda e: e.activation(
            stat[:, 67:68], stat[:, 67:68], AF.Sqrt, bias=cconst(EPS_LN))],
            reads=[b_stat, cconst_b(EPS_LN)], writes=[b_stat])
        def chain2(e):
            c = stat
            e.reciprocal(c[:, 67:68], c[:, 67:68])
            e.drain()
            if scale_col is not None:
                e.tensor_tensor(c[:, 69:70], scale_col, c[:, 67:68], ALU.mult)
            else:
                e.tensor_copy(c[:, 69:70], c[:, 67:68])
            e.drain()
            return e.scalar_tensor_tensor(
                c[:, 70:71], c[:, 66:67], -1.0, c[:, 69:70],
                ALU.mult, ALU.mult)
        p.task("v", [chain2],
               reads=[b_stat] + ([scale_buf] if scale_buf else []),
               writes=[b_stat])
        p.task("s", [lambda e: e.activation(
            scr[:, 0:D], psq[:, 0:D], AF.Identity,
            bias=stat[:, 70:71], scale=stat[:, 69:70])],
            reads=[bankq, b_stat], writes=[b_scr])
        p.task("v", [lambda e: e.tensor_tensor(
            x_res[:, i * D:(i + 1) * D], x_res[:, i * D:(i + 1) * D],
            scr[:, 0:D], ALU.add)],
            reads=[b_scr], writes=[b_xres])

    if 1 in BLOCKSET:
        AR.reset()
        xhat_fm, b_xhat = AR.alloc("xhat_fm", [128, DC * T], BF16)
        pool_b, b_poolb = AR.alloc("pool_b", [128, DC * T], BF16)
        kappa, b_kappa = AR.alloc("kappa", [128, T], F32)
        kexp_sb, b_kexp = AR.alloc("kexp_sb", [128, DC * 128], F32)
        sel_bf, b_sel = AR.alloc("sel_bf", [128, DC * 16], BF16)
        wksum, b_wksum = AR.alloc("wksum", [128, D], F32)
        wksumT, b_wksumT = AR.alloc("wksumT", [128, DC * 16], BF16)
        sends, b_send = AR.alloc("l0_send", [128, 2], F32)
        gath, b_gath = AR.alloc("l0_gath", [128, 16], F32)
        pfx, b_pfx = AR.alloc("l0_pfx", [128, 16], F32)
        wsl = {}
        for nm in ("WrT", "WvT", "WwT"):
            wsl[nm] = [AR.alloc(nm + "_sl%d" % k, [128, DC * 128], BF16)
                       for k in range(2)]
        wkmark = AR.mark()
        wk_sb, b_wk = AR.alloc("wk_sb", [128, DC * D], BF16)

        ag_in = nc.dram_tensor("l0_agin", [DC, 128, 2], F32)
        ag_out = nc.dram_tensor("l0_agout", [DC, 1024, 2], F32)

        p.task("g", [lambda e: e.dma_start(
            out=wk_sb.rearrange("p (c d) -> p c d", d=D),
            in_=dram["Wk"].rearrange("(c p) d -> p c d", p=128))],
            writes=[b_wk], inc=16)
        p.task("g", [lambda e: e.dma_start(
            out=sel_bf.rearrange("p (c h) -> p c h", h=16),
            in_=dram["selhead"].rearrange("(c p) h -> p c h", p=128))],
            writes=[b_sel], inc=16)
        p.task("y", [lambda e: e.dma_start(out=kexp_sb[0:16, :], in_=dram["kexpand"])],
               writes=[b_kexp], inc=16)

        ln_to_fm(b_xres, lambda i: x_res[:, i * D:(i + 1) * D], b_xhat, xhat_fm)
        xf = xhat_fm.rearrange("p (c t) -> p c t", t=T)

        tap("xhat", xhat_fm, b_xhat)
        wkv = wk_sb.rearrange("p (c d) -> p c d", d=D)
        def mm_wksum(e):
            r = None
            for half in range(2):
                for c in range(DC):
                    r = e.matmul(psE[0:16, half * 512:(half + 1) * 512],
                                 sel_bf[:, c * 16:(c + 1) * 16],
                                 wkv[:, c, half * 512:(half + 1) * 512],
                                 start=(c == 0), stop=(c == DC - 1))
            return r
        p.task("t", [mm_wksum], reads=[b_sel, b_wk], writes=[bE])
        p.task("v", [lambda e: e.tensor_copy(wksum[0:16, :], psE[0:16, :])],
               reads=[bE], writes=[b_wksum])
        for c in range(DC):
            ps, bank = BANKS4[c % 4]
            p.task("t", [lambda e, c=c, ps=ps: e.transpose(
                ps[:, 0:16], wksum[0:16, c * 128:(c + 1) * 128],
                ident_f[0:16, 0:16])],
                reads=[b_wksum, b_ident], writes=[bank])
            p.task("v", [lambda e, c=c, ps=ps: e.tensor_copy(
                wksumT[:, c * 16:(c + 1) * 16], ps[:, 0:16])],
                reads=[bank], writes=[b_wksumT])

        for ts in range(4):
            ps, bank = BANKS4[ts % 4]
            def mm(e, ts=ts, ps=ps):
                r = None
                for c in range(DC):
                    r = e.matmul(ps[0:16, :], wksumT[:, c * 16:(c + 1) * 16],
                                 xf[:, c, ts * 512:(ts + 1) * 512],
                                 start=(c == 0), stop=(c == DC - 1))
                return r
            p.task("t", [mm], reads=[b_wksumT, b_xhat], writes=[bank])
            p.task("v", [lambda e, ts=ts, ps=ps: e.tensor_copy(
                kappa[0:16, ts * 512:(ts + 1) * 512], ps[0:16, :])],
                reads=[bank], writes=[b_kappa])

        tap("kappa", kappa[0:16, :], b_kappa)
        AR.release(wkmark, [b_wk])
        chunkmark = AR.mark()
        wbuf, b_w = AR.alloc("l0_w", [128, T], F32)
        ubuf, b_u = AR.alloc("l0_u", [128, T], F32)
        rbuf, b_r = AR.alloc("l0_r", [128, T], F32)
        sbuf_, b_s = AR.alloc("l0_s", [128, T], F32)
        scratch, b_scratch = AR.alloc("scratch1", [128, T], F32)

        def load_wslice(nm, oc):
            (apw, bw) = wsl[nm][oc % 2]
            p.task("g", [lambda e, nm=nm, oc=oc, apw=apw: e.dma_start(
                out=apw.rearrange("p (c j) -> p c j", j=128),
                in_=dram[nm].rearrange("(c p) j -> p c j", p=128)
                [:, :, oc * 128:(oc + 1) * 128])],
                writes=[bw], inc=16)
            return apw, bw

        cur = {nm: load_wslice(nm, 0) for nm in ("WrT", "WvT", "WwT")}
        tap("wsl0", cur["WwT"][0], cur["WwT"][1])

        def proj_chunk(apw, bw, banksel):
            for ts in range(4):
                ps, bank = BANKS4[ts % 2 + banksel]
                def mm(e, ts=ts, ps=ps, apw=apw):
                    r = None
                    wv = apw.rearrange("p (c j) -> p c j", j=128)
                    for c in range(DC):
                        r = e.matmul(ps[:], wv[:, c, :],
                                     xf[:, c, ts * 512:(ts + 1) * 512],
                                     start=(c == 0), stop=(c == DC - 1))
                    return r
                p.task("t", [mm], reads=[bw, b_xhat], writes=[bank],
                       label=f"projmm_ts{ts}")
                yield (ps, bank, ts)

        for oc in range(DC):
            nxt = {nm: load_wslice(nm, oc + 1) for nm in ("WrT", "WvT", "WwT")} \
                if oc + 1 < DC else None
            for ps, bank, ts in proj_chunk(*cur["WwT"], 0):
                if TAP == "wraw" and oc == 0 and ts == 0:
                    p.task("v", [lambda e, ps=ps: e.tensor_copy(
                        scratch[:, 0:512], ps[:])], reads=[bank],
                        writes=[b_scratch])
                    tap("wraw", scratch[:, 0:512], b_scratch)
                p.task("s", [lambda e, ps=ps: e.activation(ps[:], ps[:], AF.Sigmoid)],
                       reads=[bank], writes=[bank], label=f"sig_ts{ts}")
                p.task("v", [lambda e, ps=ps, ts=ts: e.tensor_scalar(
                    wbuf[:, ts * 512:(ts + 1) * 512], ps[:], 0.99, None, ALU.mult)],
                    reads=[bank], writes=[b_w], label=f"wevac_ts{ts}")
            for ts in range(4):
                ps2, bank2 = BANKS2[ts % 2]
                p.task("t", [lambda e, ts=ts, ps2=ps2, oc=oc: e.matmul(
                    ps2[:, 0:512], kexp_sb[0:16, oc * 128:(oc + 1) * 128],
                    kappa[0:16, ts * 512:(ts + 1) * 512], start=True, stop=True)],
                    reads=[b_kexp, b_kappa], writes=[bank2])
                p.task("v", [lambda e, ps2=ps2, ts=ts: e.tensor_copy(
                    ubuf[:, ts * 512:(ts + 1) * 512], ps2[:, 0:512])],
                    reads=[bank2], writes=[b_u])
            for ps, bank, ts in proj_chunk(*cur["WvT"], 0):
                p.task("v", [lambda e, ps=ps, ts=ts: e.tensor_tensor(
                    ubuf[:, ts * 512:(ts + 1) * 512], ps[:],
                    ubuf[:, ts * 512:(ts + 1) * 512], ALU.mult)],
                    reads=[bank, b_u], writes=[b_u])
            for ps, bank, ts in proj_chunk(*cur["WrT"], 2):
                p.task("s", [lambda e, ps=ps, ts=ts: e.activation(
                    rbuf[:, ts * 512:(ts + 1) * 512], ps[:], AF.Sigmoid)],
                    reads=[bank], writes=[b_r])

            p.task("v", [lambda e: e.tensor_tensor_scan(
                sbuf_, wbuf, ubuf, 0.0, ALU.mult, ALU.add)],
                reads=[b_w, b_u], writes=[b_s])
            p.task("s", [lambda e: e.activation(
                scratch, wbuf, AF.Ln, accum_out=sends[:, 1:2])],
                reads=[b_w], writes=[b_scratch, b_send])
            p.task("v", [lambda e: e.tensor_copy(sends[:, 0:1], sbuf_[:, T - 1:T])],
                reads=[b_s], writes=[b_send])
            p.task("s", [lambda e: e.activation(
                sends[:, 1:2], sends[:, 1:2], AF.Exp)],
                reads=[b_send], writes=[b_send])
            t_send = p.task("g", [lambda e, oc=oc: e.dma_start(
                out=ag_in.ap()[oc], in_=sends)], reads=[b_send], inc=16)
            tok_cc = p.task("g", [lambda e, oc=oc: e.collective_compute(
                "AllGather", ALU.bypass, replica_groups=[list(range(NCORES))],
                ins=[ag_in.ap()[oc]], outs=[ag_out.ap()[oc]])],
                extra=[t_send], inc=1, force=True)
            t_gath = p.task("g", [lambda e, oc=oc: e.dma_start(
                out=gath.rearrange("p (r c) -> p r c", c=2),
                in_=ag_out.ap()[oc].rearrange("(r p) c -> p r c", p=128))],
                extra=[tok_cc], writes=[b_gath], inc=16)
            def comb(e):
                g = gath.rearrange("p (r c) -> p r c", c=2)
                e.memset(pfx[:, 0:1], 0.0)
                for r in range(7):
                    e.drain()
                    e.scalar_tensor_tensor(
                        pfx[:, r + 1:r + 2], pfx[:, r:r + 1],
                        g[:, r, 1:2], g[:, r, 0:1], ALU.mult, ALU.add)
                e.drain()
                e.tensor_tensor(pfx[:, 8:16], pfx[:, 0:8], cmask[:], ALU.mult)
                e.drain()
                return e.reduce_sum(pfx[:, 0:1], pfx[:, 8:16], axis=AX.X)
            p.task("v", [comb], reads=[b_gath, b_cmask], writes=[b_pfx],
                   extra=[t_gath])
            p.task("v", [lambda e: e.tensor_tensor_scan(
                sbuf_, wbuf, ubuf, pfx[:, 0:1], ALU.mult, ALU.add)],
                reads=[b_w, b_u, b_pfx], writes=[b_s])
            p.task("v", [lambda e, oc=oc: e.tensor_tensor(
                pool_b[:, oc * T:(oc + 1) * T], rbuf, sbuf_, ALU.mult)],
                reads=[b_r, b_s], writes=[b_poolb])
            if oc == 0:
                tap("w0", wbuf, b_w)
                tap("u0", ubuf, b_u)
                tap("r0", rbuf, b_r)
                tap("s0", sbuf_, b_s)
                tap("pfx0", pfx, b_pfx)
                tap("gath0", gath, b_gath)
            if nxt:
                cur = nxt

        AR.release(chunkmark, [b_w, b_u, b_r, b_s, b_scratch])
        woT, b_wo = AR.alloc("woT", [128, DC * D], BF16)
        scratch2, b_scr2 = AR.alloc("scratch2", [128, T], F32)
        rmsbuf, b_rms = AR.alloc("rmsbuf", [1, T], F32)
        rstd_tm, b_rstdtm = AR.alloc("rstd_tm", [128, 16], F32)
        p.task("g", [lambda e: e.dma_start(
            out=woT.rearrange("p (c d) -> p c d", d=D),
            in_=dram["WoT"].rearrange("(c p) d -> p c d", p=128))],
            writes=[b_wo], inc=16)
        for oc in range(DC):
            p.task("s", [lambda e, oc=oc: e.activation(
                scratch2, pool_b[:, oc * T:(oc + 1) * T], AF.Square)],
                reads=[b_poolb], writes=[b_scr2])
            for ts in range(4):
                ps, bank = BANKS4[ts]
                p.task("t", [lambda e, ts=ts, ps=ps, oc=oc: e.matmul(
                    ps[0:1, :], ones_f32[:, 0:1],
                    scratch2[:, ts * 512:(ts + 1) * 512],
                    start=(oc == 0), stop=(oc == DC - 1))],
                    reads=[b_scr2, b_ones], writes=[bank])
        tap("poolb", pool_b, b_poolb)
        rms_row = rmsbuf
        for ts in range(4):
            ps, bank = BANKS4[ts]
            p.task("v", [lambda e, ts=ts, ps=ps: e.tensor_scalar(
                rms_row[:, ts * 512:(ts + 1) * 512], ps[0:1, :],
                1.0 / D, EPS_RMS, ALU.mult, ALU.add)],
                reads=[bank], writes=[b_rms])
        p.task("s", [lambda e: e.activation(rms_row, rms_row, AF.Sqrt)],
               reads=[b_rms], writes=[b_rms])
        p.task("v", [lambda e: e.reciprocal(rms_row, rms_row)],
               reads=[b_rms], writes=[b_rms])
        tap("rmsrow", rms_row, b_rms)
        psx, bankx = BANKS4[0]
        def tp_rstd(e, psx=psx):
            r = None
            for i in range(NT):
                r = e.transpose(psx[:, i:i + 1],
                                rms_row[0:1, i * 128:(i + 1) * 128],
                                ones_f32[0:1, 0:1])
            return r
        p.task("t", [tp_rstd], reads=[b_rms, b_ones], writes=[bankx])
        p.task("v", [lambda e, psx=psx: e.tensor_copy(rstd_tm, psx[:, 0:16])],
               reads=[bankx], writes=[b_rstdtm])

        wov = woT.rearrange("p (c d) -> p c d", d=D)
        pbv = pool_b.rearrange("p (c t) -> p c t", t=T)
        for i in range(NT):
            psq, bankq = BANKS2[i % 2]
            def mm(e, i=i, psq=psq):
                r = None
                for half in range(2):
                    for c in range(DC):
                        r = e.matmul(
                            psq[:, half * 512:(half + 1) * 512],
                            pbv[:, c, i * 128:(i + 1) * 128],
                            wov[:, c, half * 512:(half + 1) * 512],
                            start=(c == 0), stop=(c == DC - 1))
                return r
            p.task("t", [mm], reads=[b_poolb, b_wo], writes=[bankq])
            ln_residual_from_psum(i, psq, bankq, rstd_tm[:, i:i + 1], b_rstdtm,
                                  scratch2, b_scr2)

    if 2 in BLOCKSET:
        AR.reset()
        HC = 4096
        HCC = HC // 128
        h2_scr = nc.dram_tensor("h2_scr", [HCC, 128, T], F32)
        b2_agin = nc.dram_tensor("b2_agin", [128, 16], F32)
        b2_agout = nc.dram_tensor("b2_agout", [1024, 16], F32)
        din("halosel", [128, 16])

        SM1, SM2, SW1, SW2, SSC = 72, 88, 104, 105, 106

        def tp_cols_to_row(cols_ap, row_ap, b_in, b_out):
            for g in range(4):
                ps2, bank2 = BANKS4[1 + g % 2]
                def tpf(e, g=g, ps2=ps2):
                    r = None
                    for k in range(4):
                        i = g * 4 + k
                        r = e.transpose(ps2[0:1, k * 128:(k + 1) * 128],
                                        cols_ap[:, i:i + 1], ident_f[:])
                    return r
                p.task("t", [tpf], reads=[b_in, b_ident], writes=[bank2])
                p.task("v", [lambda e, g=g, ps2=ps2: e.tensor_copy(
                    row_ap[0:1, g * 512:(g + 1) * 512], ps2[0:1, 0:512])],
                    reads=[bank2], writes=[b_out])

        def weight_quant(dramname, nchunks, wq_ap, b_wq, sw_col):
            W = dram[dramname].shape[1]
            qmark = AR.mark()
            nldq = 2 if W <= 1024 else 1
            wldq = [AR.alloc("wq_ld%d_%s" % (k, dramname), [128, W], F32)
                    for k in range(nldq)]
            csq, b_csq = AR.alloc("wq_cs_" + dramname, [128, 32], F32)
            for c in range(nchunks):
                apw, bw = wldq[c % nldq]
                p.task("g", [lambda e, c=c, apw=apw: e.dma_start(
                    out=apw, in_=dram[dramname][c * 128:(c + 1) * 128, :])],
                    writes=[bw], inc=16)
                p.task("v", [lambda e, c=c, apw=apw: e.reduce_sum(
                    csq[:, c:c + 1], apw, axis=AX.X,
                    apply_absolute_value=True)],
                    reads=[bw], writes=[b_csq])
            p.task("v", [lambda e, n=nchunks: e.reduce_sum(
                csq[:, 0:1], csq[:, 0:n], axis=AX.X)],
                reads=[b_csq], writes=[b_csq])
            psx, bankx = BANKS4[0]
            p.task("t", [lambda e: e.matmul(
                psx[0:1, 0:1], ones_f32[:, 0:1], csq[:, 0:1],
                start=True, stop=True)],
                reads=[b_csq, b_ones], writes=[bankx])
            def swch(e, n=nchunks):
                e.tensor_scalar(csq[0:1, 1:2], psx[0:1, 0:1],
                                1.0 / (n * 128 * W), 1e-5, ALU.mult, ALU.max)
                e.drain()
                return e.reciprocal(csq[0:1, 1:2], csq[0:1, 1:2])
            p.task("v", [swch], reads=[bankx, b_csq], writes=[b_csq])
            p.task("t", [lambda e: e.matmul(
                psx[:, 4:5], ones_row[0:1, :], csq[0:1, 1:2],
                start=True, stop=True)],
                reads=[b_csq, b_ones], writes=[bankx])
            p.task("v", [lambda e: e.tensor_copy(
                stat[:, sw_col:sw_col + 1], psx[:, 4:5])],
                reads=[bankx], writes=[b_stat])
            for c in range(nchunks):
                apw, bw = wldq[c % nldq]
                p.task("g", [lambda e, c=c, apw=apw: e.dma_start(
                    out=apw, in_=dram[dramname][c * 128:(c + 1) * 128, :])],
                    writes=[bw], inc=16)
                def q1(e, apw=apw):
                    e.tensor_scalar(apw, apw, stat[:, sw_col:sw_col + 1], RNE,
                                    ALU.mult, ALU.add)
                    e.drain()
                    return e.tensor_scalar(apw, apw, -RNE, -1.0, ALU.add,
                                           ALU.max)
                p.task("v", [q1], reads=[bw, b_stat], writes=[bw])
                p.task("v", [lambda e, c=c, apw=apw, W=W: e.tensor_scalar(
                    wq_ap[:, c * W:(c + 1) * W], apw, 1.0, None, ALU.min)],
                    reads=[bw], writes=[b_wq])
            AR.release(qmark, [w[1] for w in wldq] + [b_csq])

        nq_fm, b_nq = AR.alloc("nq_fm", [128, DC * T], BF16)
        w1q, b_w1q = AR.alloc("w1q", [128, DC * HC], BF16)
        factor_bc, b_factbc = AR.alloc("b2_factor", [128, T], F32)
        xqh, b_xqh = AR.alloc("b2_xqh", [128, 32], F32)
        hm, b_hm = AR.alloc("b2_hm", [128, 2], F32)
        gall, b_gall = AR.alloc("b2_gall", [128, 8 * 16], F32)
        hsel2, b_hsel = AR.alloc("b2_hsel", [128, 16], F32)
        csum2, b_csum2 = AR.alloc("b2_csum2", [128, 16], F32)
        xqh_bf, b_xqhbf = AR.alloc("b2_xqhbf", [128, 16], BF16)
        cw, b_cw = AR.alloc("b2_cw", [128, 32 * 3], F32)
        cb, b_cb = AR.alloc("b2_cb", [128, 32], F32)
        p.task("y", [lambda e: e.dma_start(out=hm, in_=dram["halomask"])],
               writes=[b_hm], inc=16)
        p.task("y", [lambda e: e.dma_start(out=hsel2, in_=dram["halosel"])],
               writes=[b_hsel], inc=16)
        p.task("y", [lambda e: e.dma_start(out=cw, in_=dram["convw"])],
               writes=[b_cw], inc=16)
        p.task("y", [lambda e: e.dma_start(out=cb, in_=dram["convb"])],
               writes=[b_cb], inc=16)

        weight_quant("w1T", DC, w1q, b_w1q, SW1)

        stgmark = AR.mark()
        stg2, b_stg2 = AR.alloc("b2_stage", [128, 2 * D], F32)
        rows, b_rows = AR.alloc("b2_rows", [1, T], F32)
        for i in range(NT):
            src = x_res[:, i * D:(i + 1) * D]
            p.task("v", [lambda e, src=src, i=i: e.reduce_sum(
                stat[:, i:i + 1], src, axis=AX.X)],
                reads=[b_xres], writes=[b_stat])
            p.task("s", [lambda e, src=src, i=i: e.activation(
                stg2[:, 0:D], src, AF.Square,
                accum_out=stat[:, 16 + i:16 + i + 1])],
                reads=[b_xres], writes=[b_stg2, b_stat])
        def chain(e):
            m = stat[:, 32:48]; v2 = stat[:, 48:64]
            e.tensor_scalar(m, stat[:, 0:16], 1.0 / D, None, ALU.mult)
            e.tensor_scalar(v2, stat[:, 16:32], 1.0 / D, None, ALU.mult)
            e.drain()
            e.tensor_tensor(stat[:, 0:16], m, m, ALU.mult)
            e.drain()
            return e.tensor_tensor(v2, v2, stat[:, 0:16], ALU.subtract)
        p.task("v", [chain], reads=[b_stat], writes=[b_stat])
        p.task("s", [lambda e: e.activation(
            stat[:, 48:64], stat[:, 48:64], AF.Sqrt, bias=cconst(EPS_LN))],
            reads=[b_stat, cconst_b(EPS_LN)], writes=[b_stat])
        def chain2(e):
            e.reciprocal(stat[:, 0:16], stat[:, 48:64])
            e.drain()
            return e.scalar_tensor_tensor(
                stat[:, 16:32], stat[:, 32:48], -1.0,
                stat[:, 0:16], ALU.mult, ALU.mult)
        p.task("v", [chain2], reads=[b_stat], writes=[b_stat])
        for i in range(NT):
            src = x_res[:, i * D:(i + 1) * D]
            sl = stg2[:, (i % 2) * D:(i % 2) * D + D]
            p.task("s", [lambda e, src=src, i=i, sl=sl: e.activation(
                sl, src, AF.Identity, bias=stat[:, 16 + i:16 + i + 1],
                scale=stat[:, 0 + i:0 + i + 1])],
                reads=[b_xres, b_stat], writes=[b_stg2])
            def mq(e, i=i, sl=sl):
                e.reduce_max(stat[:, SM1 + i:SM1 + i + 1], sl, axis=AX.X,
                             apply_absolute_value=True)
                e.drain()
                e.tensor_scalar(stat[:, SM1 + i:SM1 + i + 1],
                                stat[:, SM1 + i:SM1 + i + 1], 1e-5, None,
                                ALU.max)
                e.drain()
                e.reciprocal(stat[:, SSC:SSC + 1],
                             stat[:, SM1 + i:SM1 + i + 1])
                e.drain()
                return e.tensor_scalar(stat[:, SSC:SSC + 1],
                                       stat[:, SSC:SSC + 1], 127.0, None,
                                       ALU.mult)
            p.task("v", [mq], reads=[b_stg2, b_stat], writes=[b_stat])
            def qz(e, i=i, sl=sl):
                e.tensor_scalar(sl, sl, stat[:, SSC:SSC + 1], RNE, ALU.mult,
                                ALU.add)
                e.drain()
                e.tensor_scalar(sl, sl, -RNE, -128.0, ALU.add, ALU.max)
                e.drain()
                return e.tensor_scalar(sl, sl, 127.0, None, ALU.min)
            p.task("v", [qz], reads=[b_stg2, b_stat], writes=[b_stg2])
            for half in range(2):
                ps, bank = BANKS4[(2 * i + half) % 4]
                def tp(e, sl=sl, half=half, ps=ps):
                    r = None
                    for cc4 in range(4):
                        cc = half * 4 + cc4
                        r = e.transpose(ps[:, cc4 * 128:(cc4 + 1) * 128],
                                        sl[:, cc * 128:(cc + 1) * 128],
                                        ident_f[:])
                    return r
                p.task("t", [tp], reads=[b_stg2, b_ident], writes=[bank])
                def ev(e, i=i, half=half, ps=ps):
                    return e.tensor_copy(
                        nq_fm.rearrange("p (c t) -> p c t", t=T)
                        [:, half * 4:half * 4 + 4, i * 128:(i + 1) * 128],
                        ps[:].rearrange("p (c w) -> p c w", w=128))
                p.task("v", [ev], reads=[bank], writes=[b_nq])
        def fac1(e):
            e.reciprocal(stat[:, SSC:SSC + 1], stat[:, SW1:SW1 + 1])
            e.drain()
            e.tensor_scalar(stat[:, SSC:SSC + 1], stat[:, SSC:SSC + 1],
                            1.0 / 127.0, None, ALU.mult)
            e.drain()
            return e.tensor_tensor(
                stat[:, 48:64], stat[:, SM1:SM1 + 16],
                stat[:, SSC:SSC + 1].broadcast_to([128, 16]), ALU.mult)
        p.task("v", [fac1], reads=[b_stat], writes=[b_stat])
        tp_cols_to_row(stat[:, 48:64], rows[0:1, 0:T], b_stat, b_rows)
        for tsx in range(4):
            ps2, bank2 = BANKS4[2 + tsx % 2]
            p.task("t", [lambda e, tsx=tsx, ps2=ps2: e.matmul(
                ps2[:], ones_row[0:1, :],
                rows[0:1, tsx * 512:(tsx + 1) * 512], start=True, stop=True)],
                reads=[b_rows, b_ones], writes=[bank2])
            p.task("v", [lambda e, tsx=tsx, ps2=ps2: e.tensor_copy(
                factor_bc[:, tsx * 512:(tsx + 1) * 512], ps2[:])],
                reads=[bank2], writes=[b_factbc])

        nqv = nq_fm.rearrange("p (c t) -> p c t", t=T)
        def mk_send2(e):
            xv = xqh[:, 0:16].rearrange("p (c k) -> p c k", k=2)
            e.tensor_copy(xv[:, :, 0:1], nqv[:, :, 0:1])
            e.tensor_copy(xv[:, :, 1:2], nqv[:, :, T - 1:T])
            e.drain()
            e.tensor_tensor(
                xv[:, :, 0:1], xv[:, :, 0:1],
                factor_bc[:, 0:1].unsqueeze(1).broadcast_to([128, 8, 1]),
                ALU.mult)
            e.drain()
            return e.tensor_tensor(
                xv[:, :, 1:2], xv[:, :, 1:2],
                factor_bc[:, T - 1:T].unsqueeze(1).broadcast_to([128, 8, 1]),
                ALU.mult)
        p.task("v", [mk_send2], reads=[b_nq, b_factbc], writes=[b_xqh])
        t_send = p.task("g", [lambda e: e.dma_start(
            out=b2_agin.ap(), in_=xqh[:, 0:16])], reads=[b_xqh], inc=16)
        tok_cc = p.task("g", [lambda e: e.collective_compute(
            "AllGather", ALU.bypass, replica_groups=[list(range(NCORES))],
            ins=[b2_agin.ap()], outs=[b2_agout.ap()])],
            extra=[t_send], inc=1, force=True)
        t_gath = p.task("g", [lambda e: e.dma_start(
            out=gall.rearrange("p (r c) -> p r c", c=16),
            in_=b2_agout.ap().rearrange("(r p) c -> p r c", p=128))],
            extra=[tok_cc], writes=[b_gall], inc=16)
        def pick3(e):
            g3 = gall.rearrange("p (r c) -> p r c", c=16)
            xv = xqh[:, 16:32].rearrange("p (c k) -> p c k", k=2)
            for c in range(DC):
                e.tensor_tensor(csum2[:, 0:8], g3[:, :, 2 * c + 1],
                                hsel2[:, 0:8], ALU.mult)
                e.tensor_tensor(csum2[:, 8:16], g3[:, :, 2 * c],
                                hsel2[:, 8:16], ALU.mult)
                e.drain()
                e.reduce_sum(xv[:, c, 0:1], csum2[:, 0:8], axis=AX.X)
                e.reduce_sum(xv[:, c, 1:2], csum2[:, 8:16], axis=AX.X)
                e.drain()
            e.tensor_tensor(xv[:, :, 0:1], xv[:, :, 0:1],
                            hm[:, 0:1].unsqueeze(1).broadcast_to([128, 8, 1]),
                            ALU.mult)
            e.drain()
            return e.tensor_tensor(
                xv[:, :, 1:2], xv[:, :, 1:2],
                hm[:, 1:2].unsqueeze(1).broadcast_to([128, 8, 1]), ALU.mult)
        p.task("v", [pick3], reads=[b_gall, b_hsel, b_hm, b_csum2],
               writes=[b_xqh, b_csum2], extra=[t_gath])
        p.task("v", [lambda e: e.tensor_copy(xqh_bf, xqh[:, 16:32])],
               reads=[b_xqh], writes=[b_xqhbf])

        AR.release(stgmark, [b_stg2, b_rows])
        hbuf, b_hb = AR.alloc("b2_hbuf", [128, T + 2], F32)
        h2co, b_h2co = AR.alloc("b2_h2co", [128, T], F32)
        hmax, b_hmax = AR.alloc("b2_hmax", [128, T], F32)
        p.task("v", [lambda e: e.memset(hmax, 0.0)], writes=[b_hmax])
        w1qv = w1q.rearrange("p (c j) -> p c j", j=HC)
        xqv = xqh_bf.rearrange("p (c k) -> p c k", k=2)
        for co in range(HCC):
            ps2, bank2 = BANKS4[3]
            def mme(e, co=co, ps2=ps2):
                r = None
                for c in range(DC):
                    r = e.matmul(ps2[:, 0:2],
                                 w1qv[:, c, co * 128:(co + 1) * 128],
                                 xqv[:, c, :], start=(c == 0),
                                 stop=(c == DC - 1))
                return r
            p.task("t", [mme], reads=[b_w1q, b_xqhbf], writes=[bank2])
            def eve(e, ps2=ps2):
                e.tensor_copy(hbuf[:, 0:1], ps2[:, 0:1])
                return e.tensor_copy(hbuf[:, T + 1:T + 2], ps2[:, 1:2])
            p.task("v", [eve], reads=[bank2], writes=[b_hb])
            for k in range(4):
                ps, bank = BANKS4[k % 2]
                def mmh(e, co=co, k=k, ps=ps):
                    r = None
                    for c in range(DC):
                        r = e.matmul(ps[:],
                                     w1qv[:, c, co * 128:(co + 1) * 128],
                                     nqv[:, c, k * 512:(k + 1) * 512],
                                     start=(c == 0), stop=(c == DC - 1))
                    return r
                p.task("t", [mmh], reads=[b_w1q, b_nq], writes=[bank])
                p.task("v", [lambda e, k=k, ps=ps: e.tensor_tensor(
                    hbuf[:, 1 + k * 512:1 + (k + 1) * 512], ps[:],
                    factor_bc[:, k * 512:(k + 1) * 512], ALU.mult)],
                    reads=[bank, b_factbc], writes=[b_hb])
            def cv1(e, co=co):
                return e.tensor_scalar(h2co, hbuf[:, 1:T + 1],
                                       cw[:, co * 3 + 1:co * 3 + 2],
                                       cb[:, co:co + 1], ALU.mult, ALU.add)
            def cv0(e, co=co):
                return e.scalar_tensor_tensor(
                    h2co, hbuf[:, 0:T], cw[:, co * 3 + 0:co * 3 + 1],
                    h2co, ALU.mult, ALU.add)
            def cv2(e, co=co):
                return e.scalar_tensor_tensor(
                    h2co, hbuf[:, 2:T + 2], cw[:, co * 3 + 2:co * 3 + 3],
                    h2co, ALU.mult, ALU.add)
            p.task("v", [cv1], reads=[b_hb, b_cw, b_cb], writes=[b_h2co])
            p.task("v", [cv0], reads=[b_hb, b_cw, b_h2co], writes=[b_h2co])
            p.task("v", [cv2], reads=[b_hb, b_cw, b_h2co], writes=[b_h2co])
            p.task("s", [lambda e: e.activation(h2co, h2co, AF.Silu)],
                   reads=[b_h2co], writes=[b_h2co])
            p.task("s", [lambda e: e.activation(hbuf[:, 0:T], h2co, AF.Abs)],
                   reads=[b_h2co], writes=[b_hb])
            p.task("v", [lambda e: e.tensor_tensor(hmax, hbuf[:, 0:T], hmax,
                                                   ALU.max)],
                   reads=[b_hb, b_hmax], writes=[b_hmax])
            p.task("g", [lambda e, co=co: e.dma_start(
                out=h2_scr.ap()[co], in_=h2co)], reads=[b_h2co], inc=16)

        for i in range(NT):
            ps, bank = BANKS4[i % 2]
            p.task("t", [lambda e, i=i, ps=ps: e.transpose(
                ps[:, 0:128], hmax[:, i * 128:(i + 1) * 128], ident_f[:])],
                reads=[b_hmax, b_ident], writes=[bank])
            def rmx(e, i=i, ps=ps):
                e.reduce_max(stat[:, SM2 + i:SM2 + i + 1], ps[:, 0:128],
                             axis=AX.X)
                e.drain()
                return e.tensor_scalar(stat[:, SM2 + i:SM2 + i + 1],
                                       stat[:, SM2 + i:SM2 + i + 1],
                                       1e-5, None, ALU.max)
            p.task("v", [rmx], reads=[bank, b_stat], writes=[b_stat])

        AR.reset()
        w2q, b_w2q = AR.alloc("b3_w2q", [128, HCC * D], BF16)
        weight_quant("w2T", HCC, w2q, b_w2q, SW2)
        def fac2(e):
            e.reciprocal(stat[:, SSC:SSC + 1], stat[:, SW2:SW2 + 1])
            e.drain()
            e.tensor_scalar(stat[:, SSC:SSC + 1], stat[:, SSC:SSC + 1],
                            1.0 / 127.0, None, ALU.mult)
            e.drain()
            e.tensor_tensor(stat[:, 48:64], stat[:, SM2:SM2 + 16],
                            stat[:, SSC:SSC + 1].broadcast_to([128, 16]),
                            ALU.mult)
            e.drain()
            e.reciprocal(stat[:, 32:48], stat[:, SM2:SM2 + 16])
            e.drain()
            return e.tensor_scalar(stat[:, 32:48], stat[:, 32:48], 127.0,
                                   None, ALU.mult)
        p.task("v", [fac2], reads=[b_stat], writes=[b_stat])
        rows2, b_rows2 = AR.alloc("b3_rows", [1, T], F32)
        tp_cols_to_row(stat[:, 32:48], rows2[0:1, 0:T], b_stat, b_rows2)

        h2ld = [AR.alloc("b3_h2ld%d" % k, [128, HCC * 128], F32)
                for k in range(2)]
        nq2, b_nq2 = AR.alloc("b3_nq2", [128, HCC * 128], BF16)
        s2bt, b_s2bt = AR.alloc("b3_s2bt", [128, 128], F32)
        scr3, b_scr3 = AR.alloc("b3_scr", [128, HCC * 128], F32)
        scrD, b_scrD = AR.alloc("b3_scrD", [128, D], F32)
        w2qv = w2q.rearrange("p (c j) -> p c j", j=D)
        for i in range(NT):
            apld, bld = h2ld[i % 2]
            p.task("g", [lambda e, i=i, apld=apld: e.dma_start(
                out=apld.rearrange("p (c t) -> p c t", t=128),
                in_=h2_scr.ap().rearrange("c p t -> p c t")
                [:, :, i * 128:(i + 1) * 128])],
                writes=[bld], inc=16)
            ps2, bank2 = BANKS4[2]
            p.task("t", [lambda e, i=i, ps2=ps2: e.matmul(
                ps2[:, 0:128], ones_row[0:1, :],
                rows2[0:1, i * 128:(i + 1) * 128], start=True, stop=True)],
                reads=[b_rows2, b_ones], writes=[bank2])
            p.task("v", [lambda e, ps2=ps2: e.tensor_copy(s2bt,
                                                          ps2[:, 0:128])],
                   reads=[bank2], writes=[b_s2bt])
            def qz2(e, apld=apld):
                a3 = apld.rearrange("p (c t) -> p c t", t=128)
                s3 = scr3.rearrange("p (c t) -> p c t", t=128)
                e.tensor_tensor(
                    s3, a3,
                    s2bt.unsqueeze(1).broadcast_to([128, HCC, 128]), ALU.mult)
                e.drain()
                e.tensor_scalar(scr3, scr3, RNE, -RNE, ALU.add, ALU.add)
                e.drain()
                return e.tensor_scalar(scr3, scr3, -128.0, None, ALU.max)
            p.task("v", [qz2], reads=[bld, b_s2bt, b_scr3], writes=[b_scr3])
            p.task("v", [lambda e: e.tensor_scalar(nq2, scr3, 127.0, None,
                                                   ALU.min)],
                   reads=[b_scr3], writes=[b_nq2])
            psq, bankq = BANKS2[i % 2]
            def mm2(e, psq=psq):
                r = None
                for half in range(2):
                    for c in range(HCC):
                        r = e.matmul(
                            psq[:, half * 512:(half + 1) * 512],
                            nq2[:, c * 128:(c + 1) * 128],
                            w2qv[:, c, half * 512:(half + 1) * 512],
                            start=(c == 0), stop=(c == HCC - 1))
                return r
            p.task("t", [mm2], reads=[b_nq2, b_w2q], writes=[bankq])
            ln_residual_from_psum(i, psq, bankq, stat[:, 48 + i:48 + i + 1],
                                  b_stat, scrD, b_scrD)


    if 3 in BLOCKSET:
        AR.reset()
        g_agin = nc.dram_tensor("g_agin", [DC, 64, 132], F32)
        g_agout = nc.dram_tensor("g_agout", [DC, 512, 132], F32)

        xh3, b_xh3 = AR.alloc("g_xhat", [128, DC * T], BF16)
        o_tm, b_o = AR.alloc("g_otm", [128, NT * D], BF16)
        t1bf, b_t1 = AR.alloc("g_t1", [16, T], BF16)
        negb, b_negb = AR.alloc("g_negb", [128, DC], F32)
        maskT2, b_mask = AR.alloc("g_maskT2", [128, 256], BF16)
        wgk1_sb, b_wgk1 = AR.alloc("g_wgk1", [128, DC * 16], BF16)
        cols, b_cols = AR.alloc("g_cols", [128, 112], F32)

        p.task("g", [lambda e: e.dma_start(
            out=wgk1_sb.rearrange("p (c h) -> p c h", h=16),
            in_=dram["g_Wgk1"].rearrange("(c p) h -> p c h", p=128))],
            writes=[b_wgk1], inc=16)
        p.task("y", [lambda e: e.dma_start(out=negb, in_=dram["g_bgk2"])],
               writes=[b_negb], inc=16)
        p.task("v", [lambda e: e.tensor_scalar(negb, negb, -1.0, None,
                                               ALU.mult)],
               reads=[b_negb], writes=[b_negb])
        p.task("g", [lambda e: e.dma_start(out=maskT2, in_=dram["g_maskT2"])],
               writes=[b_mask], inc=16)

        ln_to_fm(b_xres, lambda i: x_res[:, i * D:(i + 1) * D], b_xh3, xh3)
        xf3 = xh3.rearrange("p (c t) -> p c t", t=T)

        for ts4 in range(4):
            ps, bank = BANKS4[ts4 % 2]
            def mmt1(e, ts4=ts4, ps=ps):
                r = None
                for c in range(DC):
                    r = e.matmul(ps[0:16, :], wgk1_sb[:, c * 16:(c + 1) * 16],
                                 xf3[:, c, ts4 * 512:(ts4 + 1) * 512],
                                 start=(c == 0), stop=(c == DC - 1))
                return r
            p.task("t", [mmt1], reads=[b_wgk1, b_xh3], writes=[bank])
            p.task("v", [lambda e, ts4=ts4, ps=ps: e.tensor_copy(
                t1bf[0:16, ts4 * 512:(ts4 + 1) * 512], ps[0:16, :])],
                reads=[bank], writes=[b_t1])

        gmark = AR.mark()
        wsl3 = {}
        for nm in ("Wq", "Wk", "Wv"):
            wsl3[nm] = AR.alloc("g_sl" + nm, [128, DC * 128], BF16)
        wgk2_sl, b_wgk2 = AR.alloc("g_slgk2", [16, 128], BF16)
        qal, b_qf = AR.alloc("g_qal", [64, 2 * T], BF16)
        kal, b_kf = AR.alloc("g_kal", [64, 2 * T], BF16)

        ktm, b_kt = AR.alloc("g_ktm", [128, T], BF16)
        vtm, b_vt = AR.alloc("g_vtm", [128, T], BF16)
        kv32, b_kv32 = AR.alloc("g_kv32", [128, T], F32)
        kal32, b_kal32 = AR.alloc("g_kal32", [64, 2 * T], F32)
        spL = stage[0:128, 0:T]
        b_spL = b_stage
        S_all, b_S = AR.alloc("g_S", [64, 128], F32)
        PTM, b_PTM = AR.alloc("g_PTM", [128, 256], BF16)
        Vt2, b_Vt2 = AR.alloc("g_Vt2", [128, 128], BF16)
        ATsb, b_AT = AR.alloc("g_ATsb", [128, 128], F32)
        Ssc, b_Ssc = AR.alloc("g_Ssc", [64, 128], F32)
        STsb, b_ST = AR.alloc("g_STsb", [64, 128], BF16)
        sends3, b_snd3 = AR.alloc("g_sends", [64, 132], F32)
        salign, b_sal = AR.alloc("g_salign", [64, 128], F32)
        gath3 = kal32[0:64, 0:8 * 132]
        b_gth3 = b_kal32
        acc3, b_acc3 = AR.alloc("g_acc", [64, 128], F32)
        Sin3, b_Sin3 = AR.alloc("g_Sin", [64, 128], F32)
        SinT, b_SinT = AR.alloc("g_SinT", [64, 128], BF16)
        fsc, b_fsc = AR.alloc("g_fsc", [128, 128], F32)
        fsc2, b_fsc2 = AR.alloc("g_fsc2", [128, 128], F32)

        NG3 = int(os.environ.get('MK_G3', str(DC)))
        G3S = int(os.environ.get('MK_G3S', '0'))
        for g in range(G3S, NG3):
            for nm in ("Wq", "Wk", "Wv"):
                apw, bw = wsl3[nm]
                p.task("g", [lambda e, nm=nm, g=g, apw=apw: e.dma_start(
                    out=apw.rearrange("p (c j) -> p c j", j=128),
                    in_=dram["g_" + nm].rearrange("(c p) j -> p c j", p=128)
                    [:, :, g * 128:(g + 1) * 128])],
                    writes=[bw], inc=16)
            p.task("g", [lambda e, g=g: e.dma_start(
                out=wgk2_sl, in_=dram["g_Wgk2"][:, g * 128:(g + 1) * 128])],
                writes=[b_wgk2], inc=16)

            wq, b_wq = wsl3["Wq"]
            wk, b_wk3 = wsl3["Wk"]
            wv, b_wv3 = wsl3["Wv"]
            for wap, bwap, dst, bdst, also32 in (
                    (wq, b_wq, qal, b_qf, None),
                    (wk, b_wk3, kal, b_kf, kal32)):
                for h in range(2):
                    for ts4 in range(4):
                        ps, bank = BANKS4[ts4 % 2]
                        def mmp(e, wap=wap, ts4=ts4, ps=ps, h=h):
                            r = None
                            wvv = wap.rearrange("p (c j) -> p c j", j=128)
                            for c in range(DC):
                                r = e.matmul(
                                    ps[0:64, :],
                                    wvv[:, c, h * 64:(h + 1) * 64],
                                    xf3[:, c, ts4 * 512:(ts4 + 1) * 512],
                                    start=(c == 0), stop=(c == DC - 1))
                            return r
                        p.task("t", [mmp], reads=[bwap, b_xh3], writes=[bank])
                        def evp(e, ts4=ts4, ps=ps, h=h, dst=dst, also32=also32):
                            r = e.tensor_copy(
                                dst[0:64, h * T + ts4 * 512:
                                    h * T + (ts4 + 1) * 512], ps[0:64, :])
                            if also32 is not None:
                                r = e.tensor_copy(
                                    also32[0:64, h * T + ts4 * 512:
                                           h * T + (ts4 + 1) * 512],
                                    ps[0:64, :])
                            return r
                        p.task("v", [evp], reads=[bank],
                               writes=[bdst] + ([b_kal32] if also32 is not None
                                                else []))
            for i in range(NT):
                for h in range(2):
                    ps, bank = BANKS4[2 + (2 * i + h) % 2]
                    p.task("t", [lambda e, i=i, h=h, ps=ps: e.transpose(
                        ps[:, 0:64],
                        kal32[0:64, h * T + i * 128:h * T + (i + 1) * 128],
                        ident_f[0:64, 0:64])],
                        reads=[b_kal32, b_ident], writes=[bank])
                    p.task("v", [lambda e, i=i, h=h, ps=ps: e.tensor_copy(
                        ktm[:, i * 128 + h * 64:i * 128 + (h + 1) * 64],
                        ps[:, 0:64])],
                        reads=[bank], writes=[b_kt])
            def projv(wap, bwap, ts4, ps, bank):
                def mmp(e, wap=wap, ts4=ts4, ps=ps):
                    r = None
                    wvv = wap.rearrange("p (c j) -> p c j", j=128)
                    for c in range(DC):
                        r = e.matmul(ps[:], wvv[:, c, :],
                                     xf3[:, c, ts4 * 512:(ts4 + 1) * 512],
                                     start=(c == 0), stop=(c == DC - 1))
                    return r
                p.task("t", [mmp], reads=[bwap, b_xh3], writes=[bank])
            for ts4 in range(4):
                ps, bank = BANKS4[ts4 % 2]
                projv(wv, b_wv3, ts4, ps, bank)
                p.task("v", [lambda e, ts4=ts4, ps=ps: e.tensor_copy(
                    kv32[:, ts4 * 512:(ts4 + 1) * 512], ps[:])],
                    reads=[bank], writes=[b_kv32])
            for i in range(NT):
                ps, bank = BANKS4[2 + i % 2]
                p.task("t", [lambda e, i=i, ps=ps: e.transpose(
                    ps[:, 0:128], kv32[:, i * 128:(i + 1) * 128], ident_f[:])],
                    reads=[b_kv32, b_ident], writes=[bank])
                p.task("v", [lambda e, i=i, ps=ps: e.tensor_copy(
                    vtm[:, i * 128:(i + 1) * 128], ps[:, 0:128])],
                    reads=[bank], writes=[b_vt])

            for ts4 in range(4):
                ps, bank = BANKS4[ts4 % 2]
                p.task("t", [lambda e, ts4=ts4, ps=ps: e.matmul(
                    ps[:], wgk2_sl,
                    t1bf[0:16, ts4 * 512:(ts4 + 1) * 512],
                    start=True, stop=True)],
                    reads=[b_wgk2, b_t1], writes=[bank])
                def spx(e, ts4=ts4, ps=ps, g=g):
                    return e.activation(
                        spL[:, ts4 * 512:(ts4 + 1) * 512], ps[:], AF.Exp,
                        scale=-1.0, bias=negb[:, g:g + 1])
                p.task("s", [spx], reads=[bank, b_negb], writes=[b_spL])
                p.task("s", [lambda e, ts4=ts4: e.activation(
                    spL[:, ts4 * 512:(ts4 + 1) * 512],
                    spL[:, ts4 * 512:(ts4 + 1) * 512], AF.Ln,
                    bias=cconst(1.0))],
                    reads=[b_spL, cconst_b(1.0)], writes=[b_spL])
            p.task("v", [lambda e: e.tensor_tensor_scan(
                spL, spL, spL, 0.0, ALU.add, ALU.max)],
                reads=[b_spL], writes=[b_spL])

            def colchain(e):
                e.memset(cols[:, 97:112], 0.0)
                e.tensor_copy(cols[:, 0:16].rearrange("p (n o) -> p n o", o=1),
                              spL.rearrange("p (n w) -> p n w", w=128)
                              [:, :, 127:128])
                e.drain()
                e.memset(cols[:, 16:17], 0.0)
                e.tensor_copy(cols[:, 17:32], cols[:, 0:15])
                e.drain()
                e.tensor_tensor(cols[:, 32:48], cols[:, 16:32], cols[:, 0:16],
                                ALU.add)
                e.tensor_tensor(cols[:, 64:80], cols[:, 16:32], cols[:, 0:16],
                                ALU.subtract)
                e.drain()
                e.tensor_scalar(cols[:, 32:48], cols[:, 32:48], 1.0 / 32.0,
                                None, ALU.mult)
                e.tensor_scalar(cols[:, 64:80], cols[:, 64:80], 1.0 / 32.0,
                                None, ALU.mult)
                e.drain()
                return e.tensor_scalar(cols[:, 48:64], cols[:, 32:48], -1.0,
                                       None, ALU.mult)
            p.task("v", [colchain], reads=[b_spL, b_cols], writes=[b_cols])
            p.task("s", [lambda e: e.activation(cols[:, 64:80], cols[:, 64:80],
                                                AF.Exp)],
                   reads=[b_cols], writes=[b_cols])
            p.task("v", [lambda e: e.tensor_tensor(
                cols[:, 80:96], cols[:, 64:80], cols[:, 64:80], ALU.mult)],
                reads=[b_cols], writes=[b_cols])
            p.task("v", [lambda e: e.tensor_copy(cols[:, 96:97],
                                                 spL[:, T - 1:T])],
                   reads=[b_spL], writes=[b_cols])
            psx4, bankx4 = BANKS4[0]
            p.task("t", [lambda e, psx4=psx4: e.transpose(
                psx4[0:64, 0:128], cols[:, 48:112], ident_f[:])],
                reads=[b_cols, b_ident], writes=[bankx4])
            p.task("v", [lambda e, psx4=psx4: e.tensor_copy(
                fsc[0:64, 0:128], psx4[0:64, 0:128])],
                   reads=[bankx4], writes=[b_fsc])
            def tpal(e, psx4=psx4):
                e.transpose(psx4[0:64, 128:192], fsc[0:64, 0:64],
                            ident_f[0:64, 0:64])
                return e.transpose(psx4[0:64, 192:256], fsc[0:64, 64:128],
                                   ident_f[0:64, 0:64])
            p.task("t", [tpal], reads=[b_fsc, b_ident], writes=[bankx4])
            def eval_(e, psx4=psx4):
                e.tensor_copy(salign[0:64, 0:64], psx4[0:64, 128:192])
                return e.tensor_copy(salign[0:64, 64:128], psx4[0:64, 192:256])
            p.task("v", [eval_], reads=[bankx4], writes=[b_sal])
            p.task("v", [lambda e: e.memset(S_all, 0.0)], writes=[b_S])

            CB3 = int(os.environ.get('MK_CB3', '5'))
            for tc in range(int(os.environ.get('MK_TC3', str(NT)))):
                boff = 0 if tc % 2 == 0 else 2
                psT1, bT1 = BANKS4[boff]
                psT2, bT2 = BANKS4[boff + 1]
                psE_, bE_ = BANKS2[tc % 2]
                def aexp(e, tc=tc):
                    return e.activation(fsc, spL[:, tc * 128:(tc + 1) * 128],
                                        AF.Exp, scale=-1.0 / 16.0,
                                        bias=cols[:, 32 + tc:33 + tc])
                p.task("s", [aexp], reads=[b_spL, b_cols], writes=[b_fsc])
                def aiexp(e, tc=tc):
                    return e.activation(fsc2, spL[:, tc * 128:(tc + 1) * 128],
                                        AF.Exp, scale=1.0 / 16.0,
                                        bias=cols[:, 48 + tc:49 + tc])
                p.task("s", [aiexp], reads=[b_spL, b_cols], writes=[b_fsc2])
                def tpa(e, psT1=psT1):
                    e.transpose(psT1[:, 0:128], fsc2, ident_f[:])
                    return e.transpose(psT1[:, 128:256], fsc, ident_f[:])
                p.task("t", [tpa], reads=[b_fsc, b_fsc2, b_ident], writes=[bT1])
                p.task("v", [lambda e, psT1=psT1:
                             e.tensor_copy(ATsb, psT1[:, 128:256])],
                       reads=[bT1], writes=[b_AT])
                p.task("v", [lambda e, tc=tc, psT1=psT1: e.tensor_tensor(
                    Vt2, vtm[:, tc * 128:(tc + 1) * 128], psT1[:, 0:128],
                    ALU.mult)],
                    reads=[bT1, b_vt], writes=[b_Vt2])
                if CB3 < 2:
                    continue
                def mmpt(e, tc=tc, psT2=psT2):
                    r = None
                    for h in range(2):
                        r = e.matmul(psT2[:, h * 128:(h + 1) * 128],
                                     kal[0:64, h * T + tc * 128:
                                         h * T + (tc + 1) * 128],
                                     qal[0:64, h * T + tc * 128:
                                         h * T + (tc + 1) * 128],
                                     start=True, stop=True)
                    return r
                p.task("t", [mmpt], reads=[b_kf, b_qf], writes=[bT2], label="mmpt")
                p.task("v", [lambda e, psT2=psT2: e.tensor_tensor(
                    PTM, psT2[:, 0:256], maskT2, ALU.mult)],
                       reads=[bT2, b_mask], writes=[b_PTM])
                if CB3 < 3:
                    continue
                def ssc_(e, tc=tc):
                    e.tensor_scalar(Ssc[0:64, 0:64], S_all[0:64, 0:64],
                                    salign[0:64, 16 + tc:17 + tc], None,
                                    ALU.mult)
                    return e.tensor_scalar(Ssc[0:64, 64:128],
                                           S_all[0:64, 64:128],
                                           salign[0:64, 80 + tc:81 + tc],
                                           None, ALU.mult)
                p.task("v", [ssc_], reads=[b_S, b_sal], writes=[b_Ssc])
                def tps(e, psE_=psE_):
                    e.transpose(psE_[0:64, 512:576], Ssc[0:64, 0:64],
                                ident_f[0:64, 0:64])
                    return e.transpose(psE_[0:64, 576:640], Ssc[0:64, 64:128],
                                       ident_f[0:64, 0:64])
                p.task("t", [tps], reads=[b_Ssc, b_ident], writes=[bE_])
                def evst(e, psE_=psE_):
                    e.tensor_copy(STsb[0:64, 0:64], psE_[0:64, 512:576])
                    return e.tensor_copy(STsb[0:64, 64:128],
                                         psE_[0:64, 576:640])
                p.task("v", [evst], reads=[bE_], writes=[b_ST])
                if CB3 < 4:
                    continue
                def mmav(e, tc=tc, psE_=psE_):
                    r = None
                    for h in range(2):
                        e.matmul(psE_[:, h * 64:(h + 1) * 64],
                                 PTM[:, h * 128:(h + 1) * 128],
                                 Vt2[:, h * 64:(h + 1) * 64],
                                 start=(h == 0), stop=False)
                        r = e.matmul(psE_[:, h * 64:(h + 1) * 64],
                                     qal[0:64, h * T + tc * 128:
                                         h * T + (tc + 1) * 128],
                                     STsb[0:64, h * 64:(h + 1) * 64],
                                     start=False, stop=(h == 1))
                    return r
                p.task("t", [mmav], reads=[b_PTM, b_Vt2, b_qf, b_ST],
                       writes=[bE_])
                p.task("v", [lambda e, tc=tc, g=g, psE_=psE_: e.tensor_tensor(
                    o_tm[:, tc * D + g * 128: tc * D + (g + 1) * 128],
                    psE_[:, 0:128], ATsb, ALU.mult)],
                    reads=[bE_, b_AT], writes=[b_o])
                if CB3 < 5:
                    continue
                def mmsadd(e, tc=tc, psT2=psT2):
                    r = None
                    for h in range(2):
                        r = e.matmul(psT2[0:64, h * 64:(h + 1) * 64],
                                     Vt2[:, h * 64:(h + 1) * 64],
                                     ktm[:, tc * 128 + h * 64:
                                         tc * 128 + (h + 1) * 64],
                                     start=(h == 0), stop=(h == 1))
                    return r
                p.task("t", [mmsadd], reads=[b_Vt2, b_kt], writes=[bT2], label="mmsadd")
                def supd(e, tc=tc, psT2=psT2):
                    e.tensor_scalar(Ssc[0:64, 0:64], psT2[0:64, 0:64],
                                    salign[0:64, 16 + tc:17 + tc], None,
                                    ALU.mult)
                    e.tensor_scalar(Ssc[0:64, 64:128], psT2[0:64, 64:128],
                                    salign[0:64, 80 + tc:81 + tc], None,
                                    ALU.mult)
                    e.drain()
                    e.scalar_tensor_tensor(
                        S_all[0:64, 0:64], S_all[0:64, 0:64],
                        salign[0:64, 32 + tc:33 + tc],
                        Ssc[0:64, 0:64], ALU.mult, ALU.add)
                    return e.scalar_tensor_tensor(
                        S_all[0:64, 64:128], S_all[0:64, 64:128],
                        salign[0:64, 96 + tc:97 + tc],
                        Ssc[0:64, 64:128], ALU.mult, ALU.add)
                p.task("v", [supd], reads=[bT2, b_sal, b_S, b_Ssc],
                       writes=[b_S, b_Ssc], label="supd")

            if os.environ.get('MK_NOX3'):
                continue
            def mkD(e):
                sv = salign.rearrange("p (h c) -> p h c", c=64)
                return e.activation(
                    sends3[0:64, 128:130].rearrange("p (h o) -> p h o", o=1),
                    sv[0:64, :, 48:49], AF.Exp, scale=-1.0 / 16.0)
            p.task("s", [mkD], reads=[b_sal], writes=[b_snd3])
            def mksend(e):
                e.tensor_copy(sends3[0:64, 0:128], S_all)
                return e.memset(sends3[0:64, 130:132], 0.0)
            p.task("v", [mksend], reads=[b_S], writes=[b_snd3])
            t_snd = p.task("g", [lambda e, g=g: e.dma_start(
                out=g_agin.ap()[g], in_=sends3[0:64, 0:132])],
                reads=[b_snd3], inc=16)
            tok_cc3 = p.task("g", [lambda e, g=g: e.collective_compute(
                "AllGather", ALU.bypass, replica_groups=[list(range(NCORES))],
                ins=[g_agin.ap()[g]], outs=[g_agout.ap()[g]])],
                extra=[t_snd], inc=1, force=True)
            t_g3 = p.task("g", [lambda e, g=g: e.dma_start(
                out=gath3[0:64, :].rearrange("p (r c) -> p r c", c=132),
                in_=g_agout.ap()[g].rearrange("(r p) c -> p r c", p=64))],
                extra=[tok_cc3], writes=[b_gth3], inc=16)
            def comb3(e):
                gv = gath3[0:64, :].rearrange("p (r c) -> p r c", c=132)
                e.memset(Sin3, 0.0)
                e.memset(acc3, 0.0)
                e.drain()
                last = None
                for r in range(NCORES):
                    last = e.scalar_tensor_tensor(Sin3, acc3,
                                                  cmask[0:64, r:r + 1], Sin3,
                                                  ALU.mult, ALU.add)
                    if r < NCORES - 1:
                        e.drain()
                        a2 = acc3.rearrange("p (h k) -> p h k", k=64)
                        e.tensor_tensor(a2, a2,
                                        gv[:, r, 128:130].unsqueeze(2)
                                        .broadcast_to([64, 2, 64]), ALU.mult)
                        e.drain()
                        e.tensor_tensor(acc3, acc3, gv[:, r, 0:128], ALU.add)
                        e.drain()
                return last
            p.task("v", [comb3], reads=[b_gth3, b_cmask, b_acc3, b_Sin3],
                   writes=[b_acc3, b_Sin3], extra=[t_g3])
            psx3, bankx3 = BANKS4[1]
            def tpsin(e, psx3=psx3):
                e.transpose(psx3[0:64, 0:64], Sin3[0:64, 0:64],
                            ident_f[0:64, 0:64])
                return e.transpose(psx3[0:64, 64:128], Sin3[0:64, 64:128],
                                   ident_f[0:64, 0:64])
            p.task("t", [tpsin], reads=[b_Sin3, b_ident], writes=[bankx3])
            p.task("v", [lambda e, psx3=psx3: e.tensor_copy(
                SinT[0:64, 0:128], psx3[0:64, 0:128])],
                   reads=[bankx3], writes=[b_SinT])
            for i in range(NT):
                psf, bf_ = BANKS4[2 + i % 2]
                p.task("t", [lambda e, i=i, psf=psf: e.transpose(
                    psf[:, 0:128], spL[:, i * 128:(i + 1) * 128], ident_f[:])],
                    reads=[b_spL, b_ident], writes=[bf_])
                p.task("s", [lambda e, psf=psf: e.activation(
                    fsc, psf[:, 0:128], AF.Exp, scale=-1.0 / 16.0)],
                    reads=[bf_], writes=[b_fsc])
                def mmfix(e, i=i, psf=psf):
                    r = None
                    for h in range(2):
                        r = e.matmul(psf[:, 128 + h * 64:128 + (h + 1) * 64],
                                     qal[0:64, h * T + i * 128:
                                         h * T + (i + 1) * 128],
                                     SinT[0:64, h * 64:(h + 1) * 64],
                                     start=(h == 0), stop=(h == 1))
                    return r
                p.task("t", [mmfix], reads=[b_qf, b_SinT], writes=[bf_])
                def fixo(e, i=i, g=g, psf=psf):
                    e.tensor_tensor(fsc2, psf[:, 128:256], fsc, ALU.mult)
                    e.drain()
                    sl = o_tm[:, i * D + g * 128: i * D + (g + 1) * 128]
                    return e.tensor_tensor(sl, sl, fsc2, ALU.add)
                p.task("v", [fixo], reads=[bf_, b_fsc, b_fsc2, b_o],
                       writes=[b_o, b_fsc2])

        AR.release(gmark, [b_qf, b_kf, b_kal32, b_kt, b_vt, b_kv32, b_spL, b_S, b_PTM,
                           b_Vt2, b_AT, b_Ssc, b_ST, b_snd3, b_gth3, b_acc3,
                           b_Sin3, b_SinT, b_fsc, b_fsc2, b_wgk2] +
                   [wsl3[nm][1] for nm in ("Wq", "Wk", "Wv")])
        wg_sb, b_wg = AR.alloc("g_wg", [128, DC * D], BF16)
        wo_sb, b_wo3 = AR.alloc("g_wo", [128, DC * D], BF16)
        gamb, b_gam = AR.alloc("g_gamb", [128, D], F32)
        betb, b_bet = AR.alloc("g_betb", [128, D], F32)
        gt_sb, b_gt = AR.alloc("g_gt", [128, D], BF16)
        og32, b_og32 = AR.alloc("g_og32", [128, D], F32)
        ogfm = [AR.alloc("g_ogfm%d" % k, [128, DC * 128], BF16)
                for k in range(2)]
        scr7, b_scr7 = AR.alloc("g_scr7", [128, D], F32)
        for nm, ap_, b_ in (("g_Wg", wg_sb, b_wg), ("g_Wout", wo_sb, b_wo3)):
            p.task("g", [lambda e, nm=nm, ap_=ap_: e.dma_start(
                out=ap_.rearrange("p (c d) -> p c d", d=D),
                in_=dram[nm].rearrange("(c p) d -> p c d", p=128))],
                writes=[b_], inc=16)
        p.task("y", [lambda e: e.dma_start(out=gamb, in_=dram["g_gamma"])],
               writes=[b_gam], inc=16)
        p.task("y", [lambda e: e.dma_start(out=betb, in_=dram["g_beta"])],
               writes=[b_bet], inc=16)
        wgv = wg_sb.rearrange("p (c d) -> p c d", d=D)
        wov3 = wo_sb.rearrange("p (c d) -> p c d", d=D)
        for i in range(NT):
            psq, bankq = BANKS2[i % 2]
            def mmg(e, i=i, psq=psq):
                r = None
                for half in range(2):
                    for c in range(DC):
                        r = e.matmul(psq[:, half * 512:(half + 1) * 512],
                                     xf3[:, c, i * 128:(i + 1) * 128],
                                     wgv[:, c, half * 512:(half + 1) * 512],
                                     start=(c == 0), stop=(c == DC - 1))
                return r
            p.task("t", [mmg], reads=[b_xh3, b_wg], writes=[bankq])
            p.task("s", [lambda e, psq=psq: e.activation(gt_sb, psq[:],
                                                         AF.Silu)],
                   reads=[bankq], writes=[b_gt])
            osl = o_tm[:, i * D:(i + 1) * D]
            p.task("v", [lambda e, osl=osl: e.reduce_sum(
                stat[:, 64:65], osl, axis=AX.X)],
                reads=[b_o], writes=[b_stat])
            p.task("s", [lambda e, osl=osl: e.activation(
                scr7, osl, AF.Square, accum_out=stat[:, 65:66])],
                reads=[b_o], writes=[b_scr7, b_stat])
            def ch6(e):
                c = stat
                e.tensor_scalar(c[:, 66:67], c[:, 64:65], 1.0 / D, None,
                                ALU.mult)
                e.tensor_scalar(c[:, 67:68], c[:, 65:66], 1.0 / D, None,
                                ALU.mult)
                e.drain()
                e.tensor_tensor(c[:, 68:69], c[:, 66:67], c[:, 66:67],
                                ALU.mult)
                e.drain()
                return e.tensor_tensor(c[:, 67:68], c[:, 67:68], c[:, 68:69],
                                       ALU.subtract)
            p.task("v", [ch6], reads=[b_stat], writes=[b_stat])
            p.task("s", [lambda e: e.activation(
                stat[:, 67:68], stat[:, 67:68], AF.Sqrt, bias=cconst(EPS_LN))],
                reads=[b_stat, cconst_b(EPS_LN)], writes=[b_stat])
            def ch7(e):
                e.reciprocal(stat[:, 69:70], stat[:, 67:68])
                e.drain()
                return e.scalar_tensor_tensor(
                    stat[:, 70:71], stat[:, 66:67], -1.0, stat[:, 69:70],
                    ALU.mult, ALU.mult)
            p.task("v", [ch7], reads=[b_stat], writes=[b_stat])
            p.task("s", [lambda e, osl=osl: e.activation(
                og32, osl, AF.Identity, bias=stat[:, 70:71],
                scale=stat[:, 69:70])],
                reads=[b_o, b_stat], writes=[b_og32])
            def gapply(e):
                e.tensor_tensor(og32, og32, gamb, ALU.mult)
                e.drain()
                e.tensor_tensor(og32, og32, betb, ALU.add)
                e.drain()
                return e.tensor_tensor(og32, og32, gt_sb, ALU.mult)
            p.task("v", [gapply], reads=[b_og32, b_gam, b_bet, b_gt],
                   writes=[b_og32])
            ogf, b_ogf = ogfm[i % 2]
            for half in range(2):
                ps, bank = BANKS4[half]
                def tpo(e, half=half, ps=ps):
                    r = None
                    for cc4 in range(4):
                        cc = half * 4 + cc4
                        r = e.transpose(ps[:, cc4 * 128:(cc4 + 1) * 128],
                                        og32[:, cc * 128:(cc + 1) * 128],
                                        ident_f[:])
                    return r
                p.task("t", [tpo], reads=[b_og32, b_ident], writes=[bank])
                p.task("v", [lambda e, half=half, ps=ps, ogf=ogf: e.tensor_copy(
                    ogf[:, half * 512:(half + 1) * 512], ps[:])],
                    reads=[bank], writes=[b_ogf])
            psq2, bankq2 = BANKS2[(i + 1) % 2]
            def mmo(e, psq2=psq2, ogf=ogf):
                r = None
                ogv = ogf.rearrange("p (c j) -> p c j", j=128)
                for half in range(2):
                    for c in range(DC):
                        r = e.matmul(psq2[:, half * 512:(half + 1) * 512],
                                     ogv[:, c, :],
                                     wov3[:, c, half * 512:(half + 1) * 512],
                                     start=(c == 0), stop=(c == DC - 1))
                return r
            p.task("t", [mmo], reads=[b_ogf, b_wo3], writes=[bankq2])
            ln_residual_from_psum(i, psq2, bankq2, None, None, scr7, b_scr7)

    if 4 in BLOCKSET:
        AR.reset()
        f_agin = nc.dram_tensor("f_agin", [128, 5120], F32)
        f_agout = nc.dram_tensor("f_agout", [1024, 5120], F32)
        f_agin1 = nc.dram_tensor("f_agin1", [1, D], F32)
        f_agout1 = nc.dram_tensor("f_agout1", [8, D], F32)
        dxa = nc.dram_tensor("f_dxa", [T, D], F32); b_dxa = Buf("dxa")
        dxb = nc.dram_tensor("f_dxb", [T, D], F32); b_dxb = Buf("dxb")
        dxc = nc.dram_tensor("f_dxc", [T, D], F32); b_dxc = Buf("dxc")

        psDb = psD[:].bitcast(BF16)

        identb, b_identb = AR.alloc("f_identb", [128, 128], BF16)
        p.task("v", [lambda e: e.tensor_copy(identb, ident_f[:])],
               reads=[b_ident], writes=[b_identb])
        onesb, b_onesb = AR.alloc("f_onesb", [128, 1], BF16)
        onesrb, b_onesrb = AR.alloc("f_onesrb", [1, 128], BF16)
        p.task("v", [lambda e: e.memset(onesb, 1.0),
                     lambda e: e.memset(onesrb, 1.0)],
               writes=[b_onesb, b_onesrb])
        hsel4, b_hsel4 = AR.alloc("f_hsel", [128, 8], F32)
        p.task("y", [lambda e: e.dma_start(out=hsel4, in_=dram["f_hsel"])],
               writes=[b_hsel4], inc=16)
        xph, b_xph = AR.alloc("f_xph", [128, 8], F32)
        g1s, b_g1s = AR.alloc("f_g1s", [8, 1024], F32)
        hselc, b_hselc = AR.alloc("f_hselc", [8, 1], F32)
        p.task("t", [lambda e: e.transpose(
            psC[0:8, 0:1], hsel4[0:1, 0:8], ident_f[0:1, 0:1])],
            reads=[b_hsel4, b_ident], writes=[bC])
        p.task("v", [lambda e: e.tensor_copy(hselc, psC[0:8, 0:1])],
               reads=[bC], writes=[b_hselc])

        def ag1(src_ap, src_buf):
            t_s = p.task("g", [lambda e: e.dma_start(
                out=f_agin1.ap(), in_=src_ap)], reads=[src_buf], inc=16)
            t_c = p.task("g", [lambda e: e.collective_compute(
                "AllGather", ALU.bypass, replica_groups=[list(range(NCORES))],
                ins=[f_agin1.ap()], outs=[f_agout1.ap()])],
                extra=[t_s], inc=1, force=True)
            t_g = p.task("g", [lambda e: e.dma_start(
                out=g1s, in_=f_agout1.ap())],
                extra=[t_c], writes=[b_g1s], inc=16)
            def selm(e):
                r = None
                for cc in range(8):
                    r = e.matmul(psC[:, cc:cc + 1],
                                 g1s[0:8, cc * 128:(cc + 1) * 128],
                                 hselc, start=(cc == 0), stop=(cc == 7))
                return r
            p.task("t", [selm], reads=[b_g1s, b_hselc], writes=[bC],
                   extra=[t_g])
            p.task("v", [lambda e: e.tensor_copy(xph, psC[:, 0:8])],
                   reads=[bC], writes=[b_xph])

        def attn_stage(l, xin_d, b_xin, xout_d, b_xout, do_ln):
            amark = AR.mark()
            wqkv_sb, b_wqkv = AR.alloc("f_wqkv_sb", [128, DC * 1536], BF16)
            wo_sb, b_wo4 = AR.alloc("f_wo_sb", [64, 8 * 1024], BF16)
            qs_t, b_qst = AR.alloc("f_qst", [128, 512], F32)
            ks_t, b_kst = AR.alloc("f_kst", [128, 512], F32)
            xwin, b_xwin = AR.alloc("f_xwin", [128, DC * 129], BF16)
            kring, b_kring = AR.alloc("f_kring", [64, 8 * 6 * 128], BF16)
            vring, b_vring = AR.alloc("f_vring", [128, 6 * 512], BF16)
            qtl, b_qtl = AR.alloc("f_qtl", [64, 8 * 128], BF16)
            rotst, b_rot = AR.alloc("f_rot_st", [128, 2048], F32)
            sscr, b_sscr = AR.alloc("f_sscr", [128, 512], F32)
            q32, b_q32 = AR.alloc("f_q32", [128, 512], F32)
            k32, b_k32 = AR.alloc("f_k32", [128, 512], F32)
            qrb, b_qrb = AR.alloc("f_qrb", [128, 512], BF16)
            krb, b_krb = AR.alloc("f_krb", [128, 512], BF16)
            Ssb, b_Ssb = AR.alloc("f_Ssb", [128, 640], F32)
            Pn, b_Pn = AR.alloc("f_Pn", [128, 640], BF16)
            PT, b_PT = AR.alloc("f_PT", [128, 256], BF16)
            otl, b_otl = AR.alloc("f_otl", [64, 8 * 128], BF16)
            xt, b_xt = AR.alloc("f_xt", [128, D], F32)
            xtb, b_xtb = AR.alloc("f_xtb", [128, D], BF16)
            xold, b_xold = AR.alloc("f_xold", [128, D], F32)
            xnew, b_xnew = AR.alloc("f_xnew", [128, D], F32)
            ghalo, b_ghalo = AR.alloc("f_ghalo", [128, 128 * 8], F32)
            mask4, b_mask4 = AR.alloc("f_mask", [128, 5 * 640], F32)
            p.task("y", [lambda e: e.dma_start(
                out=mask4.rearrange("p (m s) -> p m s", s=640),
                in_=dram["f_mask"].rearrange("m p s -> p m s"))],
                writes=[b_mask4], inc=16)
            abufs = [b_wqkv, b_wo4, b_qst, b_kst, b_xwin, b_kring, b_vring,
                     b_qtl, b_rot, b_sscr, b_q32, b_k32, b_qrb, b_krb,
                     b_Ssb, b_Pn, b_PT, b_otl, b_xt, b_xtb, b_xold, b_xnew,
                     b_ghalo, b_mask4]

            p.task("g", [lambda e, l=l: e.dma_start(
                out=wqkv_sb.rearrange("p (c j) -> p c j", j=1536),
                in_=dram["f_wqkv"][l].rearrange("(c p) j -> p c j", p=128))],
                writes=[b_wqkv], inc=16)
            p.task("g", [lambda e, l=l: e.dma_start(
                out=wo_sb.rearrange("p (h n) -> p h n", n=1024),
                in_=dram["f_wo"][l].rearrange("(h p) n -> p h n", p=64))],
                writes=[b_wo4], inc=16)
            p.task("y", [lambda e, l=l: e.dma_start(
                out=qs_t, in_=dram["f_qs"][l])], writes=[b_qst], inc=16)
            p.task("y", [lambda e, l=l: e.dma_start(
                out=ks_t, in_=dram["f_ks"][l])], writes=[b_kst], inc=16)

            if do_ln:
                for i in range(NT):
                    p.task("v", [lambda e, i=i: e.reduce_sum(
                        stat[:, i:i + 1], x_res[:, i * D:(i + 1) * D],
                        axis=AX.X)], reads=[b_xres], writes=[b_stat])
                    p.task("s", [lambda e, i=i: e.activation(
                        sscr, x_res[:, i * D:i * D + 512], AF.Square,
                        accum_out=stat[:, 16 + i:17 + i])],
                        reads=[b_xres], writes=[b_sscr, b_stat])
                    p.task("s", [lambda e, i=i: e.activation(
                        sscr, x_res[:, i * D + 512:i * D + 1024], AF.Square,
                        accum_out=stat[:, 32 + i:33 + i])],
                        reads=[b_xres], writes=[b_sscr, b_stat])
                def chain(e):
                    m = stat[:, 48:64]; v2 = stat[:, 64:80]
                    e.tensor_tensor(stat[:, 16:32], stat[:, 16:32],
                                    stat[:, 32:48], ALU.add)
                    e.tensor_scalar(m, stat[:, 0:16], 1.0 / D, None, ALU.mult)
                    e.drain()
                    e.tensor_scalar(v2, stat[:, 16:32], 1.0 / D, None,
                                    ALU.mult)
                    e.tensor_tensor(stat[:, 0:16], m, m, ALU.mult)
                    e.drain()
                    return e.tensor_tensor(v2, v2, stat[:, 0:16],
                                           ALU.subtract)
                p.task("v", [chain], reads=[b_stat], writes=[b_stat])
                p.task("s", [lambda e: e.activation(
                    stat[:, 64:80], stat[:, 64:80], AF.Sqrt,
                    bias=cconst(EPS_LN))],
                    reads=[b_stat, cconst_b(EPS_LN)], writes=[b_stat])
                def chain2(e):
                    e.reciprocal(stat[:, 0:16], stat[:, 64:80])
                    e.drain()
                    return e.scalar_tensor_tensor(
                        stat[:, 16:32], stat[:, 48:64], -1.0,
                        stat[:, 0:16], ALU.mult, ALU.mult)
                p.task("v", [chain2], reads=[b_stat], writes=[b_stat])
                for i in range(NT):
                    p.task("s", [lambda e, i=i: e.activation(
                        xt, x_res[:, i * D:(i + 1) * D], AF.Identity,
                        bias=stat[:, 16 + i:17 + i],
                        scale=stat[:, 0 + i:1 + i])],
                        reads=[b_xres, b_stat], writes=[b_xt])
                    p.task("g", [lambda e, i=i: e.dma_start(
                        out=xin_d.ap()[i * 128:(i + 1) * 128, :], in_=xt)],
                        reads=[b_xt], writes=[b_xin], inc=16)

            t_s = p.task("g", [lambda e: e.dma_start(
                out=f_agin.ap().rearrange("p (i d) -> p i d", d=1024),
                in_=xin_d.ap()[1408:2048, :].rearrange(
                    "(i p) d -> p i d", p=128))], reads=[b_xin], inc=16)
            t_c = p.task("g", [lambda e: e.collective_compute(
                "AllGather", ALU.bypass, replica_groups=[list(range(NCORES))],
                ins=[f_agin.ap()], outs=[f_agout.ap()])],
                extra=[t_s], inc=1, force=True)

            for s in range(21):
                if s < 5:
                    for cc in range(8):
                        t_g = p.task("g", [lambda e, s=s, cc=cc: e.dma_start(
                            out=ghalo.rearrange("p (r n) -> p r n", n=128),
                            in_=f_agout.ap().rearrange("(r p) n -> p r n",
                                                       p=128)
                            [:, :, s * 1024 + cc * 128:
                             s * 1024 + cc * 128 + 128])],
                            writes=[b_ghalo], extra=[t_c], inc=16)
                        def sel(e, cc=cc):
                            gv = ghalo.rearrange("p (r n) -> p r n", n=128)
                            e.tensor_tensor(
                                gv, gv, hsel4.unsqueeze(2)
                                .broadcast_to([128, 8, 128]), ALU.mult)
                            e.drain()
                            e.tensor_tensor(gv[:, 0:4, :], gv[:, 0:4, :],
                                            gv[:, 4:8, :], ALU.add)
                            e.drain()
                            e.tensor_tensor(gv[:, 0:2, :], gv[:, 0:2, :],
                                            gv[:, 2:4, :], ALU.add)
                            e.drain()
                            return e.tensor_tensor(
                                xt[:, cc * 128:(cc + 1) * 128],
                                gv[:, 0, :], gv[:, 1, :], ALU.add)
                        p.task("v", [sel], reads=[b_ghalo, b_hsel4],
                               writes=[b_xt, b_ghalo], extra=[t_g])
                    p.task("v", [lambda e: e.tensor_copy(xtb, xt)],
                           reads=[b_xt], writes=[b_xtb])
                else:
                    p.task("g", [lambda e, s=s: e.dma_start(
                        out=xtb, in_=xin_d.ap()
                        [(s - 5) * 128:(s - 4) * 128, :])],
                        reads=[b_xin], writes=[b_xtb], inc=16)
                p.task("v", [lambda e: e.tensor_copy(
                    xwin.rearrange("p (c w) -> p c w", w=129)[:, :, 0:1],
                    xwin.rearrange("p (c w) -> p c w", w=129)[:, :, 128:129])],
                    reads=[b_xwin], writes=[b_xwin])
                def tpx(e):
                    r = None
                    for c in range(8):
                        r = e.transpose(psDb[:, c * 128:(c + 1) * 128],
                                        xtb[:, c * 128:(c + 1) * 128], identb)
                    return r
                p.task("t", [tpx], reads=[b_xtb, b_identb], writes=[bD])
                p.task("v", [lambda e: e.tensor_copy(
                    xwin.rearrange("p (c w) -> p c w", w=129)[:, :, 1:129],
                    psDb.rearrange("p (c w) -> p c w", w=128))],
                    reads=[bD], writes=[b_xwin])

                if s >= 1:
                    kk = s - 1
                    slot = kk % 6
                    p.task("g", [lambda e, l=l, kk=kk: e.dma_start(
                        out=rotst, in_=dram["f_rot"][l][kk])],
                        writes=[b_rot], inc=16)
                    xv_ = xwin.rearrange("p (c w) -> p c w", w=129)
                    wv_ = wqkv_sb.rearrange("p (c j) -> p c j", j=1536)

                    def proj(e, lo, hi_ps):
                        r = None
                        for c in range(DC):
                            off = 1 if c < 4 else 0
                            r = e.matmul(hi_ps[:], xv_[:, c, off:off + 128],
                                         wv_[:, c, lo:lo + 512],
                                         start=(c == 0), stop=(c == DC - 1))
                        return r
                    p.task("t", [lambda e: proj(e, 512, psB)],
                           reads=[b_xwin, b_wqkv], writes=[bB])
                    p.task("t", [lambda e: proj(e, 1024, psC)],
                           reads=[b_xwin, b_wqkv], writes=[bC])
                    p.task("s", [lambda e: e.activation(
                        sscr, psB[:], AF.Square)], reads=[bB],
                        writes=[b_sscr])
                    p.task("v", [lambda e: e.reduce_sum(
                        stat[:, 80:88].rearrange("p (h o) -> p h o", o=1),
                        sscr.rearrange("p (h d) -> p h d", d=64),
                        axis=AX.X)], reads=[b_sscr], writes=[b_stat])
                    p.task("s", [lambda e: e.activation(
                        stat[:, 88:96], stat[:, 80:88], AF.Sqrt,
                        bias=cconst(1e-24))],
                        reads=[b_stat, cconst_b(1e-24)], writes=[b_stat])
                    p.task("v", [lambda e: e.reciprocal(
                        stat[:, 88:96], stat[:, 88:96])],
                        reads=[b_stat], writes=[b_stat])
                    def kap(e):
                        rn = stat[:, 88:96].rearrange(
                            "p (h o) -> p h o", o=1).broadcast_to(
                            [128, 8, 64])
                        e.tensor_tensor(
                            k32.rearrange("p (h d) -> p h d", d=64),
                            psB[:].rearrange("p (h d) -> p h d", d=64),
                            rn, ALU.mult)
                        e.drain()
                        return e.tensor_tensor(k32, k32, ks_t, ALU.mult)
                    p.task("v", [kap], reads=[bB, b_stat, b_kst],
                           writes=[b_k32])
                    def krot(e):
                        kC = rotst[:, 1024:1536]
                        kS3 = rotst[:, 1536:2048].rearrange(
                            "p (h d) -> p h d", d=64)
                        k3 = k32.rearrange("p (h d) -> p h d", d=64)
                        kb3 = krb.rearrange("p (h d) -> p h d", d=64)
                        e.tensor_tensor(sscr, k32, kC, ALU.mult)
                        e.tensor_tensor(kb3[:, :, 0:32], k3[:, :, 32:64],
                                        kS3[:, :, 0:32], ALU.mult)
                        e.tensor_tensor(kb3[:, :, 32:64], k3[:, :, 0:32],
                                        kS3[:, :, 32:64], ALU.mult)
                        e.drain()
                        return e.tensor_tensor(krb, krb, sscr, ALU.add)
                    p.task("v", [krot], reads=[b_k32, b_rot],
                           writes=[b_krb, b_sscr])
                    def tpk(e):
                        r = None
                        for h in range(8):
                            r = e.transpose(psDb[0:64, h * 128:(h + 1) * 128],
                                            krb[:, h * 64:(h + 1) * 64],
                                            identb)
                        return r
                    p.task("t", [tpk], reads=[b_krb, b_identb], writes=[bD])
                    def evk(e, slot=slot):
                        kv = kring.rearrange("p (h s2 w) -> p h s2 w",
                                             s2=6, w=128)
                        return e.tensor_copy(
                            kv[0:64, :, slot, :],
                            psDb[0:64, :].rearrange("p (h w) -> p h w",
                                                    w=128))
                    p.task("v", [evk], reads=[bD], writes=[b_kring])
                    p.task("v", [lambda e, slot=slot: e.tensor_copy(
                        vring.rearrange("p (s2 n) -> p s2 n", n=512)
                        [:, slot, :], psC[:])], reads=[bC],
                        writes=[b_vring])

                if s >= 5:
                    qt = s - 5
                    p.task("t", [lambda e: proj(e, 0, psA)],
                           reads=[b_xwin, b_wqkv], writes=[bA])
                    p.task("s", [lambda e: e.activation(
                        sscr, psA[:], AF.Square)], reads=[bA],
                        writes=[b_sscr])
                    p.task("v", [lambda e: e.reduce_sum(
                        stat[:, 96:104].rearrange("p (h o) -> p h o", o=1),
                        sscr.rearrange("p (h d) -> p h d", d=64),
                        axis=AX.X)], reads=[b_sscr], writes=[b_stat])
                    p.task("s", [lambda e: e.activation(
                        stat[:, 104:112], stat[:, 96:104], AF.Sqrt,
                        bias=cconst(1e-24 / 64.0), scale=1.0 / 64.0)],
                        reads=[b_stat, cconst_b(1e-24 / 64.0)],
                        writes=[b_stat])
                    p.task("v", [lambda e: e.reciprocal(
                        stat[:, 104:112], stat[:, 104:112])],
                        reads=[b_stat], writes=[b_stat])
                    def qap(e):
                        rn = stat[:, 104:112].rearrange(
                            "p (h o) -> p h o", o=1).broadcast_to(
                            [128, 8, 64])
                        e.tensor_tensor(
                            q32.rearrange("p (h d) -> p h d", d=64),
                            psA[:].rearrange("p (h d) -> p h d", d=64),
                            rn, ALU.mult)
                        e.drain()
                        return e.tensor_tensor(q32, q32, qs_t, ALU.mult)
                    p.task("v", [qap], reads=[bA, b_stat, b_qst],
                           writes=[b_q32])
                    def qrot(e):
                        qC = rotst[:, 0:512]
                        qS3 = rotst[:, 512:1024].rearrange(
                            "p (h d) -> p h d", d=64)
                        q3 = q32.rearrange("p (h d) -> p h d", d=64)
                        qb3 = qrb.rearrange("p (h d) -> p h d", d=64)
                        e.tensor_tensor(sscr, q32, qC, ALU.mult)
                        e.tensor_tensor(qb3[:, :, 0:32], q3[:, :, 32:64],
                                        qS3[:, :, 0:32], ALU.mult)
                        e.tensor_tensor(qb3[:, :, 32:64], q3[:, :, 0:32],
                                        qS3[:, :, 32:64], ALU.mult)
                        e.drain()
                        return e.tensor_tensor(qrb, qrb, sscr, ALU.add)
                    p.task("v", [qrot], reads=[b_q32, b_rot],
                           writes=[b_qrb, b_sscr])
                    def tpq(e):
                        r = None
                        for h in range(8):
                            r = e.transpose(psDb[0:64, h * 128:(h + 1) * 128],
                                            qrb[:, h * 64:(h + 1) * 64],
                                            identb)
                        return r
                    p.task("t", [tpq], reads=[b_qrb, b_identb], writes=[bD])
                    p.task("v", [lambda e: e.tensor_copy(
                        qtl.rearrange("p (h w) -> p h w", w=128)[0:64, :, :],
                        psDb[0:64, :].rearrange("p (h w) -> p h w", w=128))],
                        reads=[bD], writes=[b_qtl])

                    for h in range(8):
                        def mmS(e, h=h, qt=qt):
                            r = None
                            kv = kring.rearrange("p (h2 s2 w) -> p h2 s2 w",
                                                 s2=6, w=128)
                            qv = qtl.rearrange("p (h2 w) -> p h2 w", w=128)
                            for kc in range(5):
                                sl2 = (qt + kc) % 6
                                r = e.matmul(
                                    psE[:, kc * 128:(kc + 1) * 128],
                                    qv[0:64, h, :], kv[0:64, h, sl2, :],
                                    start=(kc == 0 or kc == 4),
                                    stop=(kc == 3 or kc == 4))
                            return r
                        p.task("t", [mmS], reads=[b_qtl, b_kring],
                               writes=[bE])
                        mi = min(qt, 4)
                        p.task("v", [lambda e, mi=mi: e.tensor_tensor(
                            Ssb, psE[:, 0:640],
                            mask4.rearrange("p (m s2) -> p m s2", s2=640)
                            [:, mi, :], ALU.add)],
                            reads=[bE, b_mask4], writes=[b_Ssb])
                        p.task("v", [lambda e: e.reduce_max(
                            stat[:, 112:113], Ssb, axis=AX.X, negate=True)],
                            reads=[b_Ssb], writes=[b_stat])
                        p.task("s", [lambda e: e.activation(
                            Pn, Ssb, AF.Exp, bias=stat[:, 112:113],
                            accum_out=stat[:, 113:114])],
                            reads=[b_Ssb, b_stat], writes=[b_Pn, b_stat])
                        p.task("v", [lambda e: e.reciprocal(
                            stat[:, 114:115], stat[:, 113:114])],
                            reads=[b_stat], writes=[b_stat])
                        p.task("s", [lambda e: e.activation(
                            Pn, Pn, AF.Identity, scale=stat[:, 114:115])],
                            reads=[b_Pn, b_stat], writes=[b_Pn])
                        for kc in range(5):
                            pp = kc % 2
                            p.task("t", [lambda e, kc=kc, pp=pp: e.transpose(
                                psDb[:, pp * 128:(pp + 1) * 128],
                                Pn[:, kc * 128:(kc + 1) * 128], identb)],
                                reads=[b_Pn, b_identb], writes=[bD])
                            p.task("v", [lambda e, pp=pp: e.tensor_copy(
                                PT[:, pp * 128:(pp + 1) * 128],
                                psDb[:, pp * 128:(pp + 1) * 128])],
                                reads=[bD], writes=[b_PT])
                            def mmAV(e, kc=kc, pp=pp, qt=qt, h=h):
                                sl2 = (qt + kc) % 6
                                vv = vring.rearrange("p (s2 n) -> p s2 n",
                                                     n=512)
                                return e.matmul(
                                    psA[0:64, 0:128],
                                    vv[:, sl2, h * 64:(h + 1) * 64],
                                    PT[:, pp * 128:(pp + 1) * 128],
                                    start=(kc == 0), stop=(kc == 4))
                            p.task("t", [mmAV], reads=[b_vring, b_PT],
                                   writes=[bA])
                        p.task("v", [lambda e, h=h: e.tensor_copy(
                            otl.rearrange("p (h2 w) -> p h2 w", w=128)
                            [0:64, h, :], psA[0:64, 0:128])],
                            reads=[bA], writes=[b_otl])
                    def mmwo(e):
                        r = None
                        ov = otl.rearrange("p (h w) -> p h w", w=128)
                        wv2 = wo_sb.rearrange("p (h n) -> p h n", n=1024)
                        for half in range(2):
                            for h in range(8):
                                r = e.matmul(
                                    psF[:, half * 512:(half + 1) * 512],
                                    ov[0:64, h, :],
                                    wv2[0:64, h, half * 512:(half + 1) * 512],
                                    start=(h == 0), stop=(h == 7))
                        return r
                    p.task("t", [mmwo], reads=[b_otl, b_wo4], writes=[bF])
                    p.task("g", [lambda e, qt=qt: e.dma_start(
                        out=xold, in_=xin_d.ap()[qt * 128:(qt + 1) * 128, :])],
                        reads=[b_xin], writes=[b_xold], inc=16)
                    p.task("v", [lambda e: e.tensor_tensor(
                        xnew, xold, psF[:], ALU.add)],
                        reads=[b_xold, bF], writes=[b_xnew])
                    p.task("g", [lambda e, qt=qt: e.dma_start(
                        out=xout_d.ap()[qt * 128:(qt + 1) * 128, :],
                        in_=xnew)], reads=[b_xnew], writes=[b_xout], inc=16)
            ag1(xnew[127:128, 0:D], b_xnew)
            AR.release(amark, abufs)

        def ffn_stage(l, xin_d, b_xin, xout_d, b_xout, final_ln):
            fmark = AR.mark()
            xh2, b_xh2 = AR.alloc("f_xh2", [128, DC * T], BF16)
            lgpp, b_lgpp = AR.alloc("f_lgpp", [128, 8], F32)
            lbpp, b_lbpp = AR.alloc("f_lbpp", [128, 8], F32)
            p.task("y", [lambda e, l=l: e.dma_start(
                out=lgpp, in_=dram["f_lg"][l])], writes=[b_lgpp], inc=16)
            p.task("y", [lambda e, l=l: e.dma_start(
                out=lbpp, in_=dram["f_lb"][l])], writes=[b_lbpp], inc=16)
            smark = AR.mark()
            x1f, b_x1f = AR.alloc("f_x1f", [128, DC * 516], BF16)
            sqg, b_sqg = AR.alloc("f_sqg", [128, DC * 512], BF16)
            rowb, b_rowb = AR.alloc("f_rowb", [1, 2048], F32)
            ftmp, b_ftmp = AR.alloc("f_ftmp", [128, 512], F32)
            xtb2, b_xtb2 = AR.alloc("f_xtb2", [128, D], BF16)
            xv4 = x1f.rearrange("p (c w) -> p c w", w=516)
            sv4 = sqg.rearrange("p (c w) -> p c w", w=512)
            hv4 = xh2.rearrange("p (c t) -> p c t", t=T)
            for tg in range(4):
                if tg == 0:
                    p.task("v", [lambda e: e.tensor_copy(
                        xv4[:, :, 0:1],
                        xph.rearrange("p (c o) -> p c o", o=1))],
                        reads=[b_xph], writes=[b_x1f])
                else:
                    p.task("v", [lambda e: e.tensor_copy(
                        xv4[:, :, 0:1], xv4[:, :, 512:513])],
                        reads=[b_x1f], writes=[b_x1f])
                for k in range(4):
                    i = tg * 4 + k
                    p.task("g", [lambda e, i=i: e.dma_start(
                        out=xtb2,
                        in_=xin_d.ap()[i * 128:(i + 1) * 128, :])],
                        reads=[b_xin], writes=[b_xtb2], inc=16)
                    def tpx2(e):
                        r = None
                        for c in range(8):
                            r = e.transpose(psDb[:, c * 128:(c + 1) * 128],
                                            xtb2[:, c * 128:(c + 1) * 128],
                                            identb)
                        return r
                    p.task("t", [tpx2], reads=[b_xtb2, b_identb], writes=[bD])
                    p.task("v", [lambda e, k=k: e.tensor_copy(
                        xv4[:, :, 1 + k * 128:1 + (k + 1) * 128],
                        psDb.rearrange("p (c w) -> p c w", w=128))],
                        reads=[bD], writes=[b_x1f])
                def sq(e):
                    e.activation(sv4[:, 0:4, :], xv4[:, 0:4, 1:513],
                                 AF.Square)
                    e.drain()
                    return e.activation(sv4[:, 4:8, :], xv4[:, 4:8, 0:512],
                                        AF.Square)
                p.task("s", [sq], reads=[b_x1f], writes=[b_sqg])
                def mmst(e):
                    r = None
                    for c in range(8):
                        off = 1 if c < 4 else 0
                        r = e.matmul(psA[0:1, :], onesb,
                                     xv4[:, c, off:off + 512],
                                     start=(c == 0), stop=(c == 7))
                    for c in range(8):
                        r = e.matmul(psB[0:1, :], onesb, sv4[:, c, :],
                                     start=(c == 0), stop=(c == 7))
                    return r
                p.task("t", [mmst], reads=[b_x1f, b_sqg, b_onesb],
                       writes=[bA, bB])
                def rchain(e):
                    e.tensor_scalar(rowb[0:1, 0:512], psA[0:1, :], 1.0 / D,
                                    None, ALU.mult)
                    e.tensor_scalar(rowb[0:1, 512:1024], psB[0:1, :],
                                    1.0 / D, None, ALU.mult)
                    e.drain()
                    e.tensor_tensor(rowb[0:1, 1024:1536], rowb[0:1, 0:512],
                                    rowb[0:1, 0:512], ALU.mult)
                    e.drain()
                    return e.tensor_tensor(
                        rowb[0:1, 512:1024], rowb[0:1, 512:1024],
                        rowb[0:1, 1024:1536], ALU.subtract)
                p.task("v", [rchain], reads=[bA, bB], writes=[b_rowb])
                p.task("s", [lambda e: e.activation(
                    rowb[0:1, 512:1024], rowb[0:1, 512:1024], AF.Sqrt,
                    bias=cconst(EPS_LN)[0:1, 0:1])],
                    reads=[b_rowb, cconst_b(EPS_LN)], writes=[b_rowb])
                p.task("v", [lambda e: e.reciprocal(
                    rowb[0:1, 1024:1536], rowb[0:1, 512:1024])],
                    reads=[b_rowb], writes=[b_rowb])
                def mmbc(e):
                    e.matmul(psE[:, 0:512], ones_row[0:1, 0:128],
                             rowb[0:1, 0:512], start=True, stop=True)
                    return e.matmul(psE[:, 512:1024], ones_row[0:1, 0:128],
                                    rowb[0:1, 1024:1536], start=True,
                                    stop=True)
                p.task("t", [mmbc], reads=[b_rowb, b_ones], writes=[bE])
                for c in range(8):
                    off = 1 if c < 4 else 0
                    def xha(e, c=c, off=off, tg=tg):
                        e.tensor_tensor(ftmp, xv4[:, c, off:off + 512],
                                        psE[:, 0:512], ALU.subtract)
                        e.drain()
                        e.tensor_tensor(ftmp, ftmp, psE[:, 512:1024],
                                        ALU.mult)
                        e.drain()
                        return e.tensor_scalar(
                            hv4[:, c, tg * 512:(tg + 1) * 512], ftmp,
                            lgpp[:, c:c + 1], lbpp[:, c:c + 1],
                            ALU.mult, ALU.add)
                    p.task("v", [xha], reads=[b_x1f, bE, b_lgpp, b_lbpp],
                           writes=[b_xh2, b_ftmp])
            AR.release(smark, [b_x1f, b_sqg, b_rowb, b_ftmp, b_xtb2])

            for pi, (j0, jn) in enumerate(((0, 1408), (1408, 1322))):
                pmark = AR.mark()
                w1a, b_w1a = AR.alloc("f_w1a", [128, DC * jn], BF16)
                w1g, b_w1g = AR.alloc("f_w1g", [128, DC * jn], BF16)
                w2s, b_w2s = AR.alloc("f_w2s", [128, 11 * 1024], BF16)
                b1a, b_b1a = AR.alloc("f_b1a", [1, jn], BF16)
                b1g, b_b1g = AR.alloc("f_b1g", [1, jn], BF16)
                b2r, b_b2r = AR.alloc("f_b2r", [1, 1024], BF16)
                actt, b_act = AR.alloc("f_actt", [128, jn], BF16)
                acT, b_acT = AR.alloc("f_acT", [128, 256], BF16)
                gelb, b_gelb = AR.alloc("f_gelb", [128, 512], BF16)
                xo2, b_xo2 = AR.alloc("f_xo2", [128, D], F32)
                xn2, b_xn2 = AR.alloc("f_xn2", [128, D], F32)
                fscr, b_fscr = xo2, b_xo2
                pbufs = [b_w1a, b_w1g, b_w2s, b_b1a, b_b1g, b_b2r, b_act,
                         b_acT, b_gelb, b_xo2, b_xn2]
                p.task("g", [lambda e, l=l, j0=j0, jn=jn, w1a=w1a: e.dma_start(
                    out=w1a.rearrange("p (c j) -> p c j", j=jn),
                    in_=dram["f_w1"][l].rearrange("(c p) j -> p c j", p=128)
                    [:, :, j0:j0 + jn])], writes=[b_w1a], inc=16)
                p.task("g", [lambda e, l=l, j0=j0, jn=jn, w1g=w1g: e.dma_start(
                    out=w1g.rearrange("p (c j) -> p c j", j=jn),
                    in_=dram["f_w1"][l].rearrange("(c p) j -> p c j", p=128)
                    [:, :, 2730 + j0:2730 + j0 + jn])],
                    writes=[b_w1g], inc=16)
                nf = jn // 128
                rem = jn % 128
                p.task("g", [lambda e, l=l, j0=j0, nf=nf, w2s=w2s: e.dma_start(
                    out=w2s.rearrange("p (c n) -> p c n", n=1024)[:, 0:nf, :],
                    in_=dram["f_w2"][l][j0:j0 + nf * 128, :].rearrange(
                        "(c p) n -> p c n", p=128))], writes=[b_w2s], inc=16)
                if rem:
                    p.task("g", [lambda e, l=l, j0=j0, nf=nf, rem=rem, w2s=w2s:
                           e.dma_start(
                        out=w2s.rearrange("p (c n) -> p c n", n=1024)
                        [0:rem, nf, :],
                        in_=dram["f_w2"][l][j0 + nf * 128:j0 + nf * 128 + rem,
                                            :])], writes=[b_w2s], inc=16)
                p.task("g", [lambda e, l=l, j0=j0, jn=jn, b1a=b1a: e.dma_start(
                    out=b1a, in_=dram["f_b1"][l:l + 1, j0:j0 + jn])],
                    writes=[b_b1a], inc=16)
                p.task("g", [lambda e, l=l, j0=j0, jn=jn, b1g=b1g: e.dma_start(
                    out=b1g, in_=dram["f_b1"]
                    [l:l + 1, 2730 + j0:2730 + j0 + jn])],
                    writes=[b_b1g], inc=16)
                p.task("g", [lambda e, l=l, b2r=b2r: e.dma_start(
                    out=b2r, in_=dram["f_b2"][l:l + 1, :])],
                    writes=[b_b2r], inc=16)
                w1av = w1a.rearrange("p (c j) -> p c j", j=jn)
                w1gv = w1g.rearrange("p (c j) -> p c j", j=jn)
                w2v = w2s.rearrange("p (c n) -> p c n", n=1024)
                ncw = (jn + 127) // 128
                for i in range(NT):
                    nj = (jn + 511) // 512
                    for jc in range(nj):
                        ja = jc * 512
                        jw = min(512, jn - ja)
                        def mm1(e, wv3, brow, ps, i=i, ja=ja, jw=jw):
                            r = None
                            for c in range(8):
                                r = e.matmul(ps[:, 0:jw],
                                             hv4[:, c, i * 128:(i + 1) * 128],
                                             wv3[:, c, ja:ja + jw],
                                             start=(c == 0), stop=False)
                            return e.matmul(ps[:, 0:jw], onesrb,
                                            brow[0:1, ja:ja + jw],
                                            start=False, stop=True)
                        p.task("t", [lambda e, jw=jw, ja=ja, i=i, mm1=mm1, w1av=w1av,
                                     b1a=b1a: mm1(e, w1av, b1a, psA, i, ja, jw)],
                               reads=[b_xh2, b_w1a, b_b1a, b_onesrb],
                               writes=[bA])
                        p.task("t", [lambda e, jw=jw, ja=ja, i=i, mm1=mm1, w1gv=w1gv,
                                     b1g=b1g: mm1(e, w1gv, b1g, psB, i, ja, jw)],
                               reads=[b_xh2, b_w1g, b_b1g, b_onesrb],
                               writes=[bB])
                        p.task("s", [lambda e, jw=jw, gelb=gelb: e.activation(
                            gelb[:, 0:jw], psB[:, 0:jw], AF.Gelu)],
                            reads=[bB], writes=[b_gelb])
                        p.task("v", [lambda e, ja=ja, jw=jw, actt=actt, gelb=gelb:
                            e.tensor_tensor(
                            actt[:, ja:ja + jw], gelb[:, 0:jw],
                            psA[:, 0:jw], ALU.mult)],
                            reads=[bA, b_gelb], writes=[b_act])
                    for c2 in range(ncw):
                        cw = min(128, jn - c2 * 128)
                        pp = c2 % 2
                        p.task("t", [lambda e, c2=c2, cw=cw, pp=pp, actt=actt:
                               e.transpose(
                            psDb[0:cw, pp * 128:(pp + 1) * 128],
                            actt[:, c2 * 128:c2 * 128 + cw], identb)],
                            reads=[b_act, b_identb], writes=[bD])
                        p.task("v", [lambda e, pp=pp, cw=cw, acT=acT: e.tensor_copy(
                            acT[0:cw, pp * 128:(pp + 1) * 128],
                            psDb[0:cw, pp * 128:(pp + 1) * 128])],
                            reads=[bD], writes=[b_acT])
                        def mmw2(e, c2=c2, cw=cw, pp=pp, acT=acT, w2v=w2v):
                            r = None
                            for half in range(2):
                                r = e.matmul(
                                    psF[:, half * 512:(half + 1) * 512],
                                    acT[0:cw, pp * 128:(pp + 1) * 128],
                                    w2v[0:cw, c2,
                                        half * 512:(half + 1) * 512],
                                    start=(c2 == 0), stop=False)
                            return r
                        p.task("t", [mmw2], reads=[b_acT, b_w2s], writes=[bF])
                    def mmb2(e, pi=pi, b2r=b2r):
                        r = None
                        for half in range(2):
                            if pi == 1:
                                r = e.matmul(
                                    psF[:, half * 512:(half + 1) * 512],
                                    onesrb, b2r[0:1,
                                                half * 512:(half + 1) * 512],
                                    start=False, stop=True)
                            else:
                                r = e.matmul(
                                    psF[:, half * 512:(half + 1) * 512],
                                    onesrb, zrow[0:1,
                                                 half * 512:(half + 1) * 512],
                                    start=False, stop=True)
                        return r
                    p.task("t", [mmb2], reads=[b_b2r, b_onesrb, b_zrow],
                           writes=[bF])
                    src_d, b_srcd = (xin_d, b_xin) if pi == 0 else (dxc,
                                                                    b_dxc)
                    dst_d, b_dstd = (dxc, b_dxc) if pi == 0 else (xout_d,
                                                                  b_xout)
                    p.task("g", [lambda e, i=i, src_d=src_d, xo2=xo2: e.dma_start(
                        out=xo2, in_=src_d.ap()[i * 128:(i + 1) * 128, :])],
                        reads=[b_srcd], writes=[b_xo2], inc=16)
                    p.task("v", [lambda e, xn2=xn2, xo2=xo2: e.tensor_tensor(
                        xn2, xo2, psF[:], ALU.add)],
                        reads=[b_xo2, bF], writes=[b_xn2])
                    if pi == 1 and final_ln:
                        ln_residual_from_psum(i, xn2, b_xn2, None, None,
                                              fscr, b_fscr)
                    else:
                        p.task("g", [lambda e, i=i, dst_d=dst_d, xn2=xn2: e.dma_start(
                            out=dst_d.ap()[i * 128:(i + 1) * 128, :],
                            in_=xn2)], reads=[b_xn2], writes=[b_dstd],
                            inc=16)
                AR.release(pmark, pbufs)
            AR.release(fmark, [b_xh2, b_lgpp, b_lbpp])

        zrow, b_zrow = AR.alloc("f_zrow", [1, 1024], BF16)
        p.task("v", [lambda e: e.memset(zrow, 0.0)], writes=[b_zrow])

        attn_stage(0, dxa, b_dxa, dxb, b_dxb, True)
        ffn_stage(0, dxb, b_dxb, dxa, b_dxa, False)
        attn_stage(1, dxa, b_dxa, dxb, b_dxb, False)
        ffn_stage(1, dxb, b_dxb, dxa, b_dxa, True)

    p.task("y", [lambda e: e.dma_start(
        out=out.rearrange("(i p) d -> p i d", p=128),
        in_=x_res.ap().rearrange("p (i d) -> p i d", d=D))],
        reads=[b_xres], inc=16)
    if os.environ.get("MK_DUMPLOG"):
        for n in ("t", "v", "s", "g", "y"):
            print(f"=== {n} ===")
            print("\n".join(p.q[n].log[:400]))
    p.q["y"].run(lambda e, p=p: e.wait_ge(p.sems["y"], p.q["y"].cnt))
    p.play()
    build.prog = p
    ctx.__exit__(None, None, None)
    return nc



def _host_inputs(inputs, core):
    c = core
    x = np.ascontiguousarray(
        np.asarray(inputs["L0_attn_x"]).reshape(16384, D)[c * T:(c + 1) * T])
    coremask = np.zeros((128, 8), np.float32)
    coremask[:, c] = 1.0
    m = {"x": x, "coremask": coremask,
         "ident": np.eye(128, dtype=np.float32)}
    if 1 in BLOCKSET:
        selhead = np.zeros((D, 16), np.float32)
        for h in range(16):
            selhead[h * 64:(h + 1) * 64, h] = 1.0
        kexpand = np.zeros((16, 128 * DC), np.float32)
        for j in range(D):
            kexpand[j // 64, j] = 1.0
        m.update({
            "selhead": selhead,
            "kexpand": kexpand,
            "WrT": np.ascontiguousarray(np.asarray(inputs["L0_attn_Wr"]).T),
            "WvT": np.ascontiguousarray(np.asarray(inputs["L0_attn_Wv"]).T),
            "WwT": np.ascontiguousarray(np.asarray(inputs["L0_attn_Ww"]).T),
            "WoT": np.ascontiguousarray(np.asarray(inputs["L0_attn_Wo"]).T),
            "Wk": np.ascontiguousarray(np.asarray(inputs["L0_attn_Wk"])),
        })
    if 2 in BLOCKSET:
        w1 = np.asarray(inputs["L0_ffn_w1"])
        w2 = np.asarray(inputs["L0_ffn_w2"])
        cwx = np.asarray(inputs["L0_ffn_conv_w"]).reshape(4096, 3)
        cbx = np.asarray(inputs["L0_ffn_conv_b"]).reshape(4096)
        convw = np.zeros((128, 96), np.float32)
        convb = np.zeros((128, 32), np.float32)
        for co in range(32):
            convw[:, co * 3:(co + 1) * 3] = cwx[co * 128:(co + 1) * 128]
            convb[:, co] = cbx[co * 128:(co + 1) * 128]
        halomask = np.ones((128, 2), np.float32)
        if c == 0:
            halomask[:, 0] = 0.0
        if c == NCORES - 1:
            halomask[:, 1] = 0.0
        halosel = np.zeros((128, 16), np.float32)
        if c > 0:
            halosel[:, c - 1] = 1.0
        if c < NCORES - 1:
            halosel[:, 8 + c + 1] = 1.0
        m.update({
            "w1T": np.ascontiguousarray(w1.T),
            "w2T": np.ascontiguousarray(w2.T),
            "convw": convw,
            "convb": convb,
            "halomask": halomask,
            "halosel": halosel,
        })
    if 3 in BLOCKSET:
        bg = np.asarray(inputs["L1_attn_bgk2"]).reshape(1024)
        bgf = np.zeros((128, DC), np.float32)
        for cc in range(DC):
            bgf[:, cc] = bg[cc * 128:(cc + 1) * 128]
        mt = np.zeros((128, 256), np.float32)
        for j in range(128):
            for ii in range(128):
                if j <= ii:
                    mt[j, ii] = 1.0
                    mt[j, 128 + ii] = 1.0
        m.update({
            "g_Wq": np.ascontiguousarray(np.asarray(inputs["L1_attn_Wq"])),
            "g_Wk": np.ascontiguousarray(np.asarray(inputs["L1_attn_Wk"])),
            "g_Wv": np.ascontiguousarray(np.asarray(inputs["L1_attn_Wv"])),
            "g_Wg": np.ascontiguousarray(np.asarray(inputs["L1_attn_Wg"])),
            "g_Wout": np.ascontiguousarray(np.asarray(inputs["L1_attn_Wout"])),
            "g_Wgk1": np.ascontiguousarray(np.asarray(inputs["L1_attn_Wgk1"])),
            "g_Wgk2": np.ascontiguousarray(np.asarray(inputs["L1_attn_Wgk2"])),
            "g_bgk2": bgf,
            "g_gamma": np.ascontiguousarray(np.broadcast_to(
                np.asarray(inputs["L1_attn_gamma"]).reshape(1, 1024),
                (128, 1024))),
            "g_beta": np.ascontiguousarray(np.broadcast_to(
                np.asarray(inputs["L1_attn_beta"]).reshape(1, 1024),
                (128, 1024))),
            "g_maskT2": mt,
        })
    if 4 in BLOCKSET:
        qs4 = np.asarray(inputs["L1_ffn_q_scale"])
        ks4 = np.asarray(inputs["L1_ffn_k_scale"])
        lg4 = np.asarray(inputs["L1_ffn_ln_g"])
        lb4 = np.asarray(inputs["L1_ffn_ln_b"])
        mask = np.zeros((5, 128, 640), np.float32)
        for mi in range(5):
            for pp_ in range(128):
                valid = np.zeros(640, bool)
                valid[pp_:pp_ + 513] = True
                if c == 0 and mi < 4:
                    valid[:512 - mi * 128] = False
                mask[mi, pp_, ~valid] = -1e30
        half_idx = np.arange(0, 64, 2, dtype=np.float64)
        inv_freq = 1.0 / 10000.0 ** (half_idx / 64.0)
        base_s = (half_idx + 0.4 * 64.0) / (1.4 * 64.0)
        rot = np.zeros((2, 20, 128, 2048), np.float32)
        for kk in range(20):
            tg = (2048 * c - 512 + kk * 128) + np.arange(128, dtype=np.float64)
            fr = tg[:, None] * inv_freq[None, :]
            power = (tg - 16384 // 2) / 256.0
            sc = base_s[None, :] ** power[:, None]
            cosf = np.cos(fr)
            sinf = np.sin(fr)
            qC = np.concatenate([cosf * sc, cosf * sc], 1)
            qS = np.concatenate([-sinf * sc, sinf * sc], 1)
            kC = np.concatenate([cosf / sc, cosf / sc], 1)
            kS = np.concatenate([-sinf / sc, sinf / sc], 1)
            rot[:, kk, :, 0:512] = np.tile(qC, (1, 8)).astype(np.float32)
            rot[:, kk, :, 512:1024] = np.tile(qS, (1, 8)).astype(np.float32)
            rot[:, kk, :, 1024:1536] = np.tile(kC, (1, 8)).astype(np.float32)
            rot[:, kk, :, 1536:2048] = np.tile(kS, (1, 8)).astype(np.float32)
        hs4 = np.zeros((128, 8), np.float32)
        if c > 0:
            hs4[:, c - 1] = 1.0
        m.update({
            "f_wqkv": np.ascontiguousarray(np.asarray(inputs["L1_ffn_wqkv"])),
            "f_wo": np.ascontiguousarray(np.asarray(inputs["L1_ffn_wo"])),
            "f_w1": np.ascontiguousarray(np.asarray(inputs["L1_ffn_w1"])),
            "f_w2": np.ascontiguousarray(np.asarray(inputs["L1_ffn_w2"])),
            "f_b1": np.ascontiguousarray(np.asarray(inputs["L1_ffn_b1"])),
            "f_b2": np.ascontiguousarray(np.asarray(inputs["L1_ffn_b2"])),
            "f_qs": np.ascontiguousarray(np.stack(
                [np.tile(qs4[l], (128, 8)) for l in range(2)])),
            "f_ks": np.ascontiguousarray(np.stack(
                [np.tile(ks4[l], (128, 8)) for l in range(2)])),
            "f_lg": np.ascontiguousarray(np.stack(
                [lg4[l].reshape(8, 128).T for l in range(2)])),
            "f_lb": np.ascontiguousarray(np.stack(
                [lb4[l].reshape(8, 128).T for l in range(2)])),
            "f_mask": mask,
            "f_rot": rot,
            "f_hsel": hs4,
        })
    return m


def kernel(**inputs):
    nc = bass.Bass("TRN2", target_bir_lowering=False)
    build(nc)
    kernel.nc_dbg = nc
    if os.environ.get("MK_BUILDONLY"):
        raise RuntimeError("buildonly")
    in_maps = [_host_inputs(inputs, c) for c in range(NCORES)]
    res = run_bass_kernel_spmd(nc, in_maps, core_ids=list(range(NCORES)))
    outp = np.concatenate([r["out"] for r in res.results], axis=0)
    if TAP:
        kernel.taps = [r["tapout"] for r in res.results]
    return outp.reshape(1, 16384, D)
